# version 09
# baseline (speedup 1.0000x reference)
import numpy as np
from contextlib import ExitStack

import concourse.bass as bass
import concourse.mybir as mybir
from concourse.bass_utils import run_bass_kernel_spmd

F32 = mybir.dt.float32
BF16 = mybir.dt.bfloat16
AF = mybir.ActivationFunctionType
ALU = mybir.AluOpType
AX = mybir.AxisListType

NPOS = 5632
HALO = 1536
NOWN = 4096
NB = 11
NCB = 8
D = 1024
DCN = 8
ECN = 16
HID = 2816
HCN = 22
GU_SEC = 256
NSEC = HID // GU_SEC
TB = 1024
NTB = NOWN // TB
NEXP = 8
EPS_LN = 1e-5
EPS_RMS = 1e-6

ENGINES = ("sync", "scalar", "vector", "tensor", "gpsimd")


class Ch:
    __slots__ = ("sem", "n")

    def __init__(self, nc, name):
        self.sem = nc.alloc_semaphore(name)
        self.n = 0


class K:
    def __init__(self, nc):
        self.nc = nc
        self.st = {e: [] for e in ENGINES}

    def raw(self, eng, f):
        self.st[eng].append(f)

    def wait(self, eng, ch, val=None):
        v = ch.n if val is None else val
        if v > 0:
            self.raw(eng, lambda e, s=ch.sem, v=v: e.wait_ge(s, v))

    def op(self, eng, f, inc=None, k=1):
        if inc is None:
            self.raw(eng, f)
            return None
        inc.n += k
        self.raw(eng, lambda e, f=f, s=inc.sem, k=k: f(e).then_inc(s, k))
        return inc.n

    def dma(self, eng, out, in_, inc, **kw):
        inc.n += 16
        self.raw(eng, lambda e, o=out, i=in_, s=inc.sem, kw=dict(kw):
                 e.dma_start(out=o, in_=i, **kw).then_inc(s, 16))
        return inc.n

    def barrier(self, chb):
        chb.n += len(ENGINES)
        tgt = chb.n
        for eng in ENGINES:
            self.raw(eng, lambda e, s=chb.sem: e.sem_inc(s, 1))
            self.raw(eng, lambda e, s=chb.sem, v=tgt: e.wait_ge(s, v))

    def replay(self):
        nc = self.nc
        with nc.Block() as block:
            @block.sync
            def _(eng):
                for f in self.st["sync"]:
                    f(eng)

            @block.scalar
            def _(eng):
                for f in self.st["scalar"]:
                    f(eng)

            @block.vector
            def _(eng):
                for f in self.st["vector"]:
                    f(eng)

            @block.tensor
            def _(eng):
                for f in self.st["tensor"]:
                    f(eng)

            @block.gpsimd
            def _(eng):
                for f in self.st["gpsimd"]:
                    f(eng)


def _ln_emit(k, ch_v, ch_a, src_view, dst_view, sm, sq_scr, eps, scale_tile=None):
    nsum = sm[:, 0:1]
    ssq = sm[:, 1:2]
    var = sm[:, 2:3]
    msq = sm[:, 3:4]
    rstd = sm[:, 4:5]
    biasv = sm[:, 5:6]
    mneg = sm[:, 6:7]
    vr = k.op("vector", lambda e: e.reduce_sum(nsum, src_view, axis=AX.X, negate=True), ch_v)
    asq = k.op("scalar", lambda e: e.activation(sq_scr, src_view, AF.Square, accum_out=ssq), ch_a)
    k.wait("vector", ch_a, asq)
    v1 = k.op("vector", lambda e: e.tensor_scalar(var, ssq, 1.0 / D, float(eps), ALU.mult, ALU.add), ch_v)
    k.wait("vector", ch_v, vr)
    v2 = k.op("vector", lambda e: e.tensor_scalar(mneg, nsum, 1.0 / D, None, ALU.mult), ch_v)
    k.wait("vector", ch_v, v2)
    v3 = k.op("vector", lambda e: e.tensor_tensor(msq, mneg, mneg, ALU.mult), ch_v)
    k.wait("vector", ch_v, v3)
    v4 = k.op("vector", lambda e: e.tensor_tensor(var, var, msq, ALU.subtract), ch_v)
    k.wait("scalar", ch_v, v4)
    a1 = k.op("scalar", lambda e: e.activation(rstd, var, AF.Ln), ch_a)
    k.wait("scalar", ch_a, a1)
    a2 = k.op("scalar", lambda e: e.activation(rstd, rstd, AF.Exp, scale=-0.5), ch_a)
    k.wait("vector", ch_a, a2)
    v5 = k.op("vector", lambda e: e.tensor_tensor(biasv, mneg, rstd, ALU.mult), ch_v)
    k.wait("scalar", ch_v, v5)
    k.wait("scalar", ch_a, a2)
    if scale_tile is None:
        a3 = k.op("scalar", lambda e: e.activation(dst_view, src_view, AF.Identity, bias=biasv, scale=rstd), ch_a)
        return a3, v5
    a3 = k.op("scalar", lambda e: e.activation(sq_scr, src_view, AF.Identity, bias=biasv, scale=rstd), ch_a)
    k.wait("vector", ch_a, a3)
    v6 = k.op("vector", lambda e: e.tensor_tensor(dst_view, sq_scr, scale_tile, ALU.mult), ch_v)
    return a3, v6


def _ln_emit4(k, ch_v, ch_a, srcs, dsts, sm, sq_scr, eps):
    n = len(srcs)
    nsum = sm[:, 0:n]
    ssq = sm[:, 4:4 + n]
    var = sm[:, 8:8 + n]
    mneg = sm[:, 12:12 + n]
    msq = sm[:, 16:16 + n]
    rstd = sm[:, 20:20 + n]
    biasv = sm[:, 24:24 + n]
    vr = 0
    for g in range(n):
        vr = k.op("vector", lambda e, g=g: e.reduce_sum(sm[:, g:g + 1], srcs[g], axis=AX.X, negate=True), ch_v)
    asq = 0
    for g in range(n):
        asq = k.op("scalar", lambda e, g=g: e.activation(sq_scr[:, g * 1024:(g + 1) * 1024], srcs[g], AF.Square,
                                                         accum_out=sm[:, 4 + g:5 + g]), ch_a)
    k.wait("vector", ch_a, asq)
    v1 = k.op("vector", lambda e: e.tensor_scalar(var, ssq, 1.0 / D, float(eps), ALU.mult, ALU.add), ch_v)
    k.wait("vector", ch_v, vr)
    v2 = k.op("vector", lambda e: e.tensor_scalar(mneg, nsum, 1.0 / D, None, ALU.mult), ch_v)
    k.wait("vector", ch_v, v2)
    v3 = k.op("vector", lambda e: e.tensor_tensor(msq, mneg, mneg, ALU.mult), ch_v)
    k.wait("vector", ch_v, v3)
    v4 = k.op("vector", lambda e: e.tensor_tensor(var, var, msq, ALU.subtract), ch_v)
    k.wait("scalar", ch_v, v4)
    a1 = k.op("scalar", lambda e: e.activation(rstd, var, AF.Ln), ch_a)
    k.wait("scalar", ch_a, a1)
    a2 = k.op("scalar", lambda e: e.activation(rstd, rstd, AF.Exp, scale=-0.5), ch_a)
    k.wait("vector", ch_a, a2)
    v5 = k.op("vector", lambda e: e.tensor_tensor(biasv, mneg, rstd, ALU.mult), ch_v)
    k.wait("scalar", ch_v, v5)
    a3 = 0
    for g in range(n):
        a3 = k.op("scalar", lambda e, g=g: e.activation(dsts[g], srcs[g], AF.Identity,
                                                        bias=sm[:, 24 + g:25 + g],
                                                        scale=sm[:, 20 + g:21 + g]), ch_a)
    return a3


def build_nc(debug=False, phases=("0", "AB", "C", "D"), ab_stage=5, nb=NB,
             ntb=NTB, nexp=NEXP, d_stage=4):
    nc = bass.Bass("TRN2", target_bir_lowering=False)

    xs = nc.dram_tensor("xs", [NPOS, D], F32, kind="ExternalInput")
    pmask = nc.dram_tensor("pmask", [128, NPOS // 128], F32, kind="ExternalInput")
    w_in_sel = nc.dram_tensor("w_in_sel", [D, 2048], F32, kind="ExternalInput")
    b_in_sel = nc.dram_tensor("b_in_sel", [128, ECN], F32, kind="ExternalInput")
    wconv_sel = nc.dram_tensor("wconv_sel", [128, ECN * 3], F32, kind="ExternalInput")
    b_conv_sel = nc.dram_tensor("b_conv_sel", [128, ECN], F32, kind="ExternalInput")
    b1c = nc.dram_tensor("b1c", [64, 1], F32, kind="ExternalInput")
    b2c = nc.dram_tensor("b2c", [64, 1], F32, kind="ExternalInput")
    w2i = nc.dram_tensor("w2i", [64, 64], F32, kind="ExternalInput")
    w3i = nc.dram_tensor("w3i", [64, D], F32, kind="ExternalInput")
    fb_pc = nc.dram_tensor("fb_pc", [128, 8], F32, kind="ExternalInput")
    w_out_i = nc.dram_tensor("w_out_i", [D, D], F32, kind="ExternalInput")
    b_out_i = nc.dram_tensor("b_out_i", [D], F32, kind="ExternalInput")
    norm_w_i = nc.dram_tensor("norm_w_i", [D], F32, kind="ExternalInput")
    router_T = nc.dram_tensor("router_T", [D, 8], F32, kind="ExternalInput")
    gu_T = nc.dram_tensor("gu_T", [NEXP, D, 2 * HID], F32, kind="ExternalInput")
    dn_T = nc.dram_tensor("dn_T", [NEXP, HID, D], F32, kind="ExternalInput")
    alpha_i = nc.dram_tensor("alpha_i", [1, 1], F32, kind="ExternalInput")
    rev_i = nc.dram_tensor("rev_i", [128, 128], F32, kind="ExternalInput")
    diagw = nc.dram_tensor("diagw", [ECN * 3, 128, 128], F32, kind="ExternalInput")
    flag_i = nc.dram_tensor("flag_i", [1, 1], F32, kind="ExternalInput")

    y_out = nc.dram_tensor("y_out", [NOWN, D], F32, kind="ExternalOutput")

    skind = "ExternalOutput" if debug else "Internal"
    s_hpad = nc.dram_tensor("s_hpad", [2176], F32, kind=skind)
    s_v1 = nc.dram_tensor("s_v1", [D, NPOS], BF16, kind=skind)
    s_s0 = nc.dram_tensor("s_s0", [D, NPOS], BF16, kind=skind)
    s_x1 = nc.dram_tensor("s_x1", [NOWN, D], F32, kind=skind)
    s_xn2 = nc.dram_tensor("s_xn2", [NOWN, D], F32, kind=skind)
    s_normed = nc.dram_tensor("s_normed", [NOWN, D], BF16, kind=skind)
    if debug:
        s_y = nc.dram_tensor("s_y", [D, NOWN], BF16, kind="ExternalOutput")
        s_attn = nc.dram_tensor("s_attn", [NOWN, D], F32, kind="ExternalOutput")

    k = K(nc)

    chb = Ch(nc, "bar")
    ch_v = Ch(nc, "v")
    ch_a = Ch(nc, "a")
    ch_p = Ch(nc, "p")
    ch_pe = Ch(nc, "pe")
    ch_c0 = Ch(nc, "c0")
    ch_wu = Ch(nc, "wu")
    ch_xP = [Ch(nc, "xA"), Ch(nc, "xB")]
    ch_v1stP = [Ch(nc, "v1stA"), Ch(nc, "v1stB")]
    ch_s0stP = [Ch(nc, "s0stA"), Ch(nc, "s0stB")]
    ch_hp = Ch(nc, "hp")

    es_res = ExitStack()

    G = es_res.enter_context(nc.sbuf_tensor("G", [128, 2048], BF16))
    cb_in = es_res.enter_context(nc.sbuf_tensor("cb_in", [128, ECN], F32))
    cwconv = es_res.enter_context(nc.sbuf_tensor("cwconv", [128, ECN * 3], F32))
    cb_conv = es_res.enter_context(nc.sbuf_tensor("cb_conv", [128, ECN], F32))
    cfb = es_res.enter_context(nc.sbuf_tensor("cfb", [128, 8], F32))
    cpm = es_res.enter_context(nc.sbuf_tensor("cpm", [128, NPOS // 128], F32))
    cbout = es_res.enter_context(nc.sbuf_tensor("cbout", [128, D], F32))
    cnormw = es_res.enter_context(nc.sbuf_tensor("cnormw", [128, D], F32))
    calpha = es_res.enter_context(nc.sbuf_tensor("calpha", [128, 1], F32))
    crouter = es_res.enter_context(nc.sbuf_tensor("crouter", [128, DCN * 8], BF16))
    cflag = es_res.enter_context(nc.sbuf_tensor("cflag", [128, 1], F32))

    k.dma("sync", cb_in[:, :], b_in_sel.ap(), ch_c0)
    k.dma("sync", cwconv[:, :], wconv_sel.ap(), ch_c0)
    k.dma("sync", cb_conv[:, :], b_conv_sel.ap(), ch_c0)
    k.dma("sync", cfb[:, :], fb_pc.ap(), ch_c0)
    k.dma("sync", cpm[:, :], pmask.ap(), ch_c0)
    k.dma("sync", cbout[:, :], bass.AP(b_out_i, 0, [[0, 128], [1, D]]), ch_c0)
    k.dma("sync", cnormw[:, :], bass.AP(norm_w_i, 0, [[0, 128], [1, D]]), ch_c0)
    k.dma("sync", calpha[:, :], bass.AP(alpha_i, 0, [[0, 128], [1, 1]]), ch_c0)
    k.dma("sync", cflag[:, :], bass.AP(flag_i, 0, [[0, 128], [1, 1]]), ch_c0)
    k.dma("gpsimd", crouter[:, :], bass.AP(router_T, 0, [[8, 128], [1024, DCN], [1, 8]]), ch_wu)

    es0 = ExitStack()
    t_b1 = es0.enter_context(nc.sbuf_tensor("t_b1", [64, 1], F32))
    t_b2 = es0.enter_context(nc.sbuf_tensor("t_b2", [64, 1], F32))
    t_w2 = es0.enter_context(nc.sbuf_tensor("t_w2", [64, 64], F32))
    t_w3 = es0.enter_context(nc.sbuf_tensor("t_w3", [64, D], F32))
    t_rev = es0.enter_context(nc.sbuf_tensor("t_rev", [128, 128], F32))
    t_s1 = es0.enter_context(nc.sbuf_tensor("t_s1", [64, 1], F32))
    t_a = es0.enter_context(nc.sbuf_tensor("t_a", [64, 1], F32))
    t_hb = es0.enter_context(nc.sbuf_tensor("t_hb", [1, 2176], F32))
    t_hank = es0.enter_context(nc.sbuf_tensor("t_hank", [128, 2048], F32))
    p_h1 = es0.enter_context(nc.psum_tensor("p_h1", [64, 1], F32))
    p_h2 = es0.enter_context(nc.psum_tensor("p_h2", [1, D], F32))
    p_fl = es0.enter_context(nc.psum_tensor("p_fl", [128, 512], F32))

    k.dma("sync", t_b1[:, :], b1c.ap(), ch_c0)
    k.dma("sync", t_b2[:, :], b2c.ap(), ch_c0)
    k.dma("sync", t_w2[:, :], w2i.ap(), ch_c0)
    k.dma("sync", t_w3[:, :], w3i.ap(), ch_c0)
    k.dma("sync", t_rev[:, :], rev_i.ap(), ch_c0)

    k.wait("scalar", ch_c0)
    a1 = k.op("scalar", lambda e: e.activation(t_s1[:, :], t_b1[:, :], AF.Silu), ch_a)
    k.wait("tensor", ch_a, a1)
    k.wait("tensor", ch_c0)
    m1 = k.op("tensor", lambda e: e.matmul(p_h1[:, :], t_w2[:, :], t_s1[:, :], start=True, stop=True), ch_pe)
    k.wait("vector", ch_pe, m1)
    k.wait("vector", ch_c0)
    v1_ = k.op("vector", lambda e: e.tensor_add(t_a[:, :], p_h1[:, :], t_b2[:, :]), ch_v)
    k.wait("scalar", ch_v, v1_)
    a2 = k.op("scalar", lambda e: e.activation(t_a[:, :], t_a[:, :], AF.Silu), ch_a)
    vz = k.op("vector", lambda e: e.memset(t_hb[:, :], 0.0), ch_v)
    k.wait("tensor", ch_a, a2)
    k.op("tensor", lambda e: e.matmul(p_h2[:, 0:512], t_a[:, :], t_w3[:, 0:512], start=True, stop=True), ch_pe)
    m3 = k.op("tensor", lambda e: e.matmul(p_h2[:, 512:1024], t_a[:, :], t_w3[:, 512:1024], start=True, stop=True), ch_pe)
    k.wait("vector", ch_pe, m3)
    k.wait("vector", ch_v, vz)
    vh = k.op("vector", lambda e: e.tensor_copy(t_hb[:, 512:1536], p_h2[:, :]), ch_v)
    k.wait("sync", ch_v, vh)
    k.dma("sync", s_hpad.ap(), t_hb[:, :], ch_hp)
    k.wait("sync", ch_hp)
    for t0 in range(0, 2048, 512):
        k.dma("sync", t_hank[:, t0:t0 + 512], bass.AP(s_hpad, 1 + t0, [[1, 128], [1, 512]]), ch_hp)
    k.wait("tensor", ch_hp)
    for t0 in range(0, 2048, 512):
        mf = k.op("tensor", lambda e, t0=t0: e.matmul(p_fl[:, :], t_rev[:, :], t_hank[:, t0:t0 + 512],
                                                      start=True, stop=True), ch_pe)
        k.wait("scalar", ch_pe, mf)
        ag = k.op("scalar", lambda e, t0=t0: e.activation(G[:, t0:t0 + 512], p_fl[:, :], AF.Copy), ch_a)
        if t0 + 512 < 2048:
            k.wait("tensor", ch_a, ag)

    k.wait("sync", ch_c0)
    k.wait("sync", ch_hp)
    es0.close()
    k.barrier(chb)

    if "AB" in phases:
        esAB = ExitStack()
        win = esAB.enter_context(nc.sbuf_tensor("win", [128, DCN * 2048], BF16))
        cdiag = esAB.enter_context(nc.sbuf_tensor("cdiag", [128, ECN * 3 * 128], BF16))
        ubias = esAB.enter_context(nc.sbuf_tensor("ubias", [128, ECN], F32))
        wsum = esAB.enter_context(nc.sbuf_tensor("wsum", [128, ECN], F32))
        xa = [esAB.enter_context(nc.sbuf_tensor(f"xa{i}", [128, 4096], F32)) for i in range(2)]
        lnb = [esAB.enter_context(nc.sbuf_tensor(f"lnb{i}", [128, 4096], BF16)) for i in range(2)]
        lnf = [esAB.enter_context(nc.sbuf_tensor(f"lnf{i}", [128, DCN * 512], BF16)) for i in range(2)]
        zt = esAB.enter_context(nc.sbuf_tensor("zt", [128, ECN * 516], BF16))
        u0b = [esAB.enter_context(nc.sbuf_tensor(f"u0b{i}", [128, 8 * 512], BF16)) for i in range(2)]
        u2t = esAB.enter_context(nc.sbuf_tensor("u2t", [128, 512], BF16))
        v1b = [esAB.enter_context(nc.sbuf_tensor(f"v1b{i}", [128, 8 * 512], BF16)) for i in range(2)]
        smAB = esAB.enter_context(nc.sbuf_tensor("smAB", [128, 4 * 8], F32))
        sqAB = esAB.enter_context(nc.sbuf_tensor("sqAB", [128, 4096], F32))
        ident = esAB.enter_context(nc.sbuf_tensor("ident", [128, 128], BF16))
        pz = [esAB.enter_context(nc.psum_tensor(f"pz{i}", [128, 512], F32)) for i in range(2)]
        pu = [esAB.enter_context(nc.psum_tensor(f"pu{i}", [128, 512], F32)) for i in range(2)]
        ptr = [esAB.enter_context(nc.psum_tensor(f"ptr{i}", [128, 512], BF16)) for i in range(2)]

        for dc in range(DCN):
            k.dma("gpsimd", win[:, dc * 2048:(dc + 1) * 2048],
                  w_in_sel[dc * 128:(dc + 1) * 128, :], ch_wu)
        k.dma("gpsimd", cdiag[:, :], bass.AP(diagw, 0, [[128, 128], [16384, ECN * 3], [1, 128]]), ch_wu)
        wu_win = ch_wu.n

        k.wait("vector", ch_c0)
        s1v = k.op("vector", lambda e: e.tensor_tensor(
            wsum[:, :], bass.AP(cwconv, 0, [[ECN * 3, 128], [3, ECN]]),
            bass.AP(cwconv, 1, [[ECN * 3, 128], [3, ECN]]), ALU.add), ch_v)
        k.wait("vector", ch_v, s1v)
        s2v = k.op("vector", lambda e: e.tensor_tensor(
            wsum[:, :], wsum[:, :], bass.AP(cwconv, 2, [[ECN * 3, 128], [3, ECN]]), ALU.add), ch_v)
        k.wait("vector", ch_v, s2v)
        s3v = k.op("vector", lambda e: e.tensor_tensor(wsum[:, :], wsum[:, :], cb_in[:, :], ALU.mult), ch_v)
        k.wait("vector", ch_v, s3v)
        ubv = k.op("vector", lambda e: e.tensor_tensor(ubias[:, :], wsum[:, :], cb_conv[:, :], ALU.add), ch_v)
        deln = esAB.enter_context(nc.sbuf_tensor("deln", [128, 2 * ECN], F32))
        k.wait("vector", ch_v, ubv)
        f1 = k.op("vector", lambda e: e.tensor_tensor(
            deln[:, 0:ECN], bass.AP(cwconv, 0, [[ECN * 3, 128], [3, ECN]]),
            bass.AP(cwconv, 1, [[ECN * 3, 128], [3, ECN]]), ALU.add), ch_v)
        k.wait("vector", ch_v, f1)
        f2 = k.op("vector", lambda e: e.tensor_tensor(deln[:, 0:ECN], deln[:, 0:ECN], cb_in[:, :], ALU.mult), ch_v)
        k.wait("vector", ch_v, f2)
        f3 = k.op("vector", lambda e: e.tensor_scalar(deln[:, 0:ECN], deln[:, 0:ECN], -1.0, None, ALU.mult), ch_v)
        k.wait("vector", ch_v, f3)
        f4 = k.op("vector", lambda e: e.tensor_tensor(
            deln[:, ECN:2 * ECN], bass.AP(cwconv, 0, [[ECN * 3, 128], [3, ECN]]), cb_in[:, :], ALU.mult), ch_v)
        k.wait("vector", ch_v, f4)
        f5 = k.op("vector", lambda e: e.tensor_scalar(deln[:, ECN:2 * ECN], deln[:, ECN:2 * ECN], -1.0, None, ALU.mult), ch_v)
        k.wait("vector", ch_v, f5)
        k.wait("vector", ch_c0)
        f6 = k.op("vector", lambda e: e.tensor_single_scalar(deln[:, :], deln[:, :], cflag[:, 0:1], ALU.mult), ch_v)

        pz0 = k.op("gpsimd", lambda e: e.memset(zt[:, :], 0.0), ch_p)
        idm = k.op("gpsimd", lambda e: e.memset(ident[:, :], 0.0), ch_p)
        k.wait("gpsimd", ch_p, idm)
        idv = k.op("gpsimd", lambda e: e.affine_select(
            out=ident[:, :], in_=ident[:, :], compare_op=ALU.not_equal,
            fill=1.0, base=0, pattern=[[-1, 128]], channel_multiplier=1), ch_p)
        k.wait("tensor", ch_p, idv)

        zev = {}
        uev = {}
        carry = {}
        tapmm = {}
        v1mul = {}
        ab_act_done = [0] * NB
        ab_z_done = [0] * NB
        ab_taps_v = [0] * NB

        k.wait("tensor", ch_wu, wu_win)
        k.wait("scalar", ch_p, pz0)
        k.wait("scalar", ch_v, ubv)

        pz_last = {0: 0, 1: 0}
        pu_last = {0: 0, 1: 0}
        ptr_last = {0: 0, 1: 0}
        ab_tr_done = [0] * NB

        for b in range(nb):
            sl = b % 2
            if b >= 2:
                k.wait("sync", ch_a, ab_act_done[b - 2])
            k.dma("sync", xa[sl][:, :],
                  bass.AP(xs, b * 512 * D, [[D, 128], [128 * D, 4], [1, D]]), ch_xP[sl])
            xv = ch_xP[sl].n
            if b >= 2:
                k.wait("scalar", ch_pe, ab_tr_done[b - 2])
            k.wait("vector", ch_xP[sl], xv)
            k.wait("scalar", ch_xP[sl], xv)
            _ln_emit4(k, ch_v, ch_a,
                      [xa[sl][:, g * 1024:(g + 1) * 1024] for g in range(4)],
                      [lnb[sl][:, g * 1024:(g + 1) * 1024] for g in range(4)],
                      smAB[:, :], sqAB[:, :], EPS_LN)
            ab_act_done[b] = ch_a.n

            if ab_stage < 2:
                continue
            k.wait("tensor", ch_a, ab_act_done[b])
            if b >= 2:
                k.wait("vector", ch_pe, ab_z_done[b - 2])
            for dc in range(DCN):
                bank = dc % 2
                k.wait("tensor", ch_v, ptr_last[bank])
                for g in range(4):
                    k.op("tensor",
                         lambda e, dc=dc, g=g, bank=bank, sl=sl:
                         e.transpose(ptr[bank][:, g * 128:(g + 1) * 128],
                                     lnb[sl][:, g * 1024 + dc * 128: g * 1024 + dc * 128 + 128],
                                     ident[:, :]),
                         ch_pe if g == 3 else None)
                trv = ch_pe.n
                k.wait("vector", ch_pe, trv)
                ev = k.op("vector", lambda e, dc=dc, bank=bank, sl=sl:
                          e.tensor_copy(lnf[sl][:, dc * 512:(dc + 1) * 512], ptr[bank][:, :]), ch_v)
                ptr_last[bank] = ev
            ab_tr_done[b] = ch_pe.n

            k.wait("tensor", ch_v, ch_v.n)
            for ec in range(ECN):
                bank = ec % 2
                k.wait("tensor", ch_v, pz_last[bank])
                for dc in range(DCN):
                    last = dc == DCN - 1
                    k.op("tensor",
                         lambda e, dc=dc, ec=ec, bank=bank, sl=sl, last=last:
                         e.matmul(pz[bank][:, :],
                                  win[:, dc * 2048 + ec * 128: dc * 2048 + ec * 128 + 128],
                                  lnf[sl][:, dc * 512:(dc + 1) * 512],
                                  start=(dc == 0), stop=last),
                         ch_pe if last else None)
                k.wait("vector", ch_pe, ch_pe.n)
                if b > 0:
                    k.wait("vector", ch_p, carry.get((b, ec), 0))
                    k.wait("vector", ch_pe, tapmm.get((b - 1, ec), 0))
                ae = k.op("vector",
                          lambda e, ec=ec, bank=bank:
                          e.tensor_copy(zt[:, ec * 516 + 2: ec * 516 + 514], pz[bank][:, :]),
                          ch_v)
                zev[(b, ec)] = ae
                pz_last[bank] = ae
            ab_z_done[b] = ch_pe.n

            if ab_stage < 3:
                continue
            if b >= 2:
                k.wait("vector", ch_s0stP[sl])
                k.wait("vector", ch_v1stP[sl])
            for pr in range(8):
                for side, ec in ((0, pr), (2, pr + 8)):
                    bank = (pr * 2 + (side // 2)) % 2
                    k.wait("tensor", ch_v, zev[(b, ec)])
                    k.wait("tensor", ch_v, pu_last[bank])
                    if b > 0:
                        k.wait("tensor", ch_p, carry.get((b, ec), 0))
                    for kk in range(3):
                        last = kk == 2
                        k.op("tensor",
                             lambda e, ec=ec, kk=kk, bank=bank, last=last:
                             e.matmul(pu[bank][:, :],
                                      cdiag[:, (ec * 3 + kk) * 128:(ec * 3 + kk + 1) * 128],
                                      zt[:, ec * 516 + kk: ec * 516 + kk + 512],
                                      start=(kk == 0), stop=last),
                             ch_pe if last else None)
                    tapmm[(b, ec)] = ch_pe.n
                    k.wait("vector", ch_pe, ch_pe.n)
                    if side == 0:
                        dstu = u0b[sl][:, pr * 512:(pr + 1) * 512]
                    else:
                        dstu = u2t[:, :]
                        if pr > 0:
                            k.wait("vector", ch_v, v1mul[(b, pr - 1)])
                        elif b > 0:
                            k.wait("vector", ch_v, v1mul[(b - 1, 7)])
                    au = k.op("vector",
                              lambda e, ec=ec, bank=bank, dstu=dstu:
                              e.tensor_single_scalar(dstu, pu[bank][:, :], ubias[:, ec:ec + 1], ALU.add),
                              ch_v)
                    uev[(b, ec)] = au
                    pu_last[bank] = au
                    if b == HALO // 512:
                        k.wait("vector", ch_v, au)
                        fxa = k.op("vector", lambda e, dstu=dstu, ec=ec:
                                   e.tensor_tensor(dstu[:, 0:1], dstu[:, 0:1], deln[:, ec:ec + 1], ALU.add), ch_v)
                        k.wait("vector", ch_v, fxa)
                        fxb = k.op("vector", lambda e, dstu=dstu, ec=ec:
                                   e.tensor_tensor(dstu[:, 1:2], dstu[:, 1:2], deln[:, ECN + ec:ECN + ec + 1], ALU.add), ch_v)
                        k.wait("vector", ch_v, fxb)
                if ab_stage < 4:
                    v1mul[(b, pr)] = ch_v.n
                    continue
                k.wait("vector", ch_v, uev[(b, pr + 8)])
                vv = k.op("vector", lambda e, pr=pr, sl=sl:
                          e.tensor_tensor(v1b[sl][:, pr * 512:(pr + 1) * 512],
                                          u0b[sl][:, pr * 512:(pr + 1) * 512], u2t[:, :], ALU.mult), ch_v)
                v1mul[(b, pr)] = vv
            ab_taps_v[b] = ch_v.n

            if b + 1 < nb:
                for ec in range(ECN):
                    k.wait("gpsimd", ch_pe, tapmm[(b, ec)])
                    cc_ = k.op("gpsimd", lambda e, ec=ec:
                               e.tensor_copy(zt[:, ec * 516: ec * 516 + 2],
                                             zt[:, ec * 516 + 512: ec * 516 + 514]), ch_p)
                    carry[(b + 1, ec)] = cc_

            if ab_stage >= 5:
                k.wait("gpsimd", ch_v, ab_taps_v[b])
                k.dma("gpsimd", bass.AP(s_v1, b * 512, [[NPOS, 128], [NPOS * 128, 8], [1, 512]]),
                      v1b[sl][:, :], ch_v1stP[sl])
                k.wait("gpsimd", ch_v, uev[(b, 7)])
                k.dma("gpsimd", bass.AP(s_s0, b * 512, [[NPOS, 128], [NPOS * 128, 8], [1, 512]]),
                      u0b[sl][:, :], ch_s0stP[sl])

        k.wait("sync", ch_v1stP[0])
        k.wait("sync", ch_v1stP[1])
        k.wait("sync", ch_xP[0])
        k.wait("sync", ch_xP[1])
        k.wait("sync", ch_s0stP[0])
        k.wait("sync", ch_s0stP[1])
        k.wait("gpsimd", ch_wu)
        esAB.close()
        k.barrier(chb)

    ch_ringP = [Ch(nc, "ringA"), Ch(nc, "ringB")]
    ch_vfP = [Ch(nc, "vfA"), Ch(nc, "vfB")]
    ch_sfP = [Ch(nc, "sfA"), Ch(nc, "sfB")]
    ch_xrP = [Ch(nc, "xrA"), Ch(nc, "xrB")]
    ch_dbg = Ch(nc, "dbg")
    ch_x1stP = [Ch(nc, "x1stA"), Ch(nc, "x1stB")]
    ch_xn2stP = [Ch(nc, "xn2stA"), Ch(nc, "xn2stB")]
    ch_nstP = [Ch(nc, "nstA"), Ch(nc, "nstB")]
    ch_outP = [Ch(nc, "outA"), Ch(nc, "outB")]

    if "C" in phases:
        esC = ExitStack()
        wout = esC.enter_context(nc.sbuf_tensor("wout", [128, DCN * 1024], BF16))
        ring = esC.enter_context(nc.sbuf_tensor("ring", [128, 16 * 1024], BF16))
        v1f = [esC.enter_context(nc.sbuf_tensor(f"v1f{i}", [128, 8 * 512], BF16)) for i in range(2)]
        s0f = [esC.enter_context(nc.sbuf_tensor(f"s0f{i}", [128, 8 * 512], BF16)) for i in range(2)]
        xr = [esC.enter_context(nc.sbuf_tensor(f"xr{i}", [128, 4096], F32)) for i in range(2)]
        yf = [esC.enter_context(nc.sbuf_tensor(f"yf{i}", [128, 8 * 512], BF16)) for i in range(2)]
        tf = esC.enter_context(nc.sbuf_tensor("tf", [128, 512], F32))
        lnt = [esC.enter_context(nc.sbuf_tensor(f"lnt{i}", [128, 1024], F32)) for i in range(2)]
        aot = [esC.enter_context(nc.sbuf_tensor(f"aot{i}", [128, 1024], F32)) for i in range(2)]
        x1t = [esC.enter_context(nc.sbuf_tensor(f"x1t{i}", [128, 1024], F32)) for i in range(2)]
        xint = [esC.enter_context(nc.sbuf_tensor(f"xint{i}", [128, 1024], F32)) for i in range(2)]
        nrt = [esC.enter_context(nc.sbuf_tensor(f"nrt{i}", [128, 1024], BF16)) for i in range(2)]
        smC = esC.enter_context(nc.sbuf_tensor("smC", [128, 4 * 8], F32))
        sqC = [esC.enter_context(nc.sbuf_tensor(f"sqC{i}", [128, 1024], F32)) for i in range(2)]
        pcv = [esC.enter_context(nc.psum_tensor(f"pcv{i}", [128, 512], F32)) for i in range(2)]
        po = [esC.enter_context(nc.psum_tensor(f"po{i}", [128, 512], F32)) for i in range(4)]

        for cc in range(DCN):
            k.dma("gpsimd", wout[:, cc * 1024:(cc + 1) * 1024],
                  w_out_i[cc * 128:(cc + 1) * 128, :], ch_wu)
        wu_wout = ch_wu.n
        k.wait("tensor", ch_wu, wu_wout)

        ring_mask = {}
        tf_last = 0
        conv_done = [0] * NCB
        cv_last = {0: 0, 1: 0}
        po_last = {0: 0, 1: 0, 2: 0, 3: 0}
        epi_v = [0] * NCB
        epi_a = [0] * NCB
        atnLN_a = {}
        xinLN_a = {}

        def load_ring_group(jcs, par):
            ch = ch_ringP[par]
            for jc in jcs:
                slot = jc % 16
                k.dma("sync", ring[:, slot * 1024:(slot + 1) * 1024],
                      bass.AP(s_v1, jc * 128, [[NPOS, 1024], [1, 128]]),
                      ch, transpose=True)
            k.wait("vector", ch, ch.n)
            for jc in jcs:
                slot = jc % 16
                mv = k.op("vector", lambda e, slot=slot, jc=jc:
                          e.tensor_single_scalar(ring[:, slot * 1024:(slot + 1) * 1024],
                                                 ring[:, slot * 1024:(slot + 1) * 1024],
                                                 cpm[:, jc:jc + 1], ALU.mult), ch_v)
                ring_mask[jc] = mv

        load_ring_group(range(4, 16), 0)

        for bi in range(NCB):
            sl = bi % 2
            l0 = HALO + 512 * bi
            if bi >= 1:
                if bi >= 2:
                    k.wait("sync", ch_pe, conv_done[bi - 2])
                load_ring_group(range(12 + 4 * bi, 16 + 4 * bi), bi % 2)
            if bi >= 2:
                k.wait("sync", ch_v, epi_v[bi - 2])
            k.dma("sync", v1f[sl][:, :], bass.AP(s_v1, l0, [[NPOS, 128], [NPOS * 128, 8], [1, 512]]), ch_vfP[sl])
            k.dma("sync", s0f[sl][:, :], bass.AP(s_s0, l0, [[NPOS, 128], [NPOS * 128, 8], [1, 512]]), ch_sfP[sl])
            vfv, sfv = ch_vfP[sl].n, ch_sfP[sl].n
            if bi >= 2:
                k.wait("scalar", ch_v, epi_v[bi - 2])
                k.wait("scalar", ch_a, epi_a[bi - 2])
            k.dma("scalar", xr[sl][:, :], bass.AP(xs, l0 * D, [[D, 128], [128 * D, 4], [1, D]]), ch_xrP[sl])
            xrv = ch_xrP[sl].n

            jlist = list(range((l0 - 1023) // 128, (l0 + 511) // 128 + 1))
            for jc in jlist:
                k.wait("tensor", ch_v, ring_mask[jc])
            for cc in range(DCN):
                bank = cc % 2
                k.wait("tensor", ch_v, cv_last[bank])
                for i, jc in enumerate(jlist):
                    off = l0 - 128 * jc + 384
                    ilast = i == len(jlist) - 1
                    k.op("tensor",
                         lambda e, jc=jc, cc=cc, off=off, bank=bank, i=i, ilast=ilast:
                         e.matmul(pcv[bank][:, :],
                                  ring[:, (jc % 16) * 1024 + cc * 128:(jc % 16) * 1024 + cc * 128 + 128],
                                  G[:, off:off + 512],
                                  start=(i == 0), stop=ilast),
                         ch_pe if ilast else None)
                mmv = ch_pe.n
                k.wait("vector", ch_pe, mmv)
                k.wait("vector", ch_vfP[sl], vfv)
                k.wait("vector", ch_sfP[sl], sfv)
                k.wait("vector", ch_v, tf_last)
                e1 = k.op("vector", lambda e, cc=cc, bank=bank, sl=sl:
                          e.scalar_tensor_tensor(tf[:, :], v1f[sl][:, cc * 512:(cc + 1) * 512],
                                                 cfb[:, cc:cc + 1], pcv[bank][:, :],
                                                 ALU.mult, ALU.add), ch_v)
                k.wait("vector", ch_v, e1)
                e2 = k.op("vector", lambda e, cc=cc, sl=sl:
                          e.tensor_tensor(yf[sl][:, cc * 512:(cc + 1) * 512], tf[:, :],
                                          s0f[sl][:, cc * 512:(cc + 1) * 512], ALU.mult), ch_v)
                cv_last[bank] = e2
                tf_last = e2
            conv_done[bi] = ch_pe.n
            yf_done = ch_v.n
            if debug:
                k.wait("sync", ch_v, yf_done)
                k.dma("sync", bass.AP(s_y, bi * 512, [[NOWN, 128], [NOWN * 128, 8], [1, 512]]),
                      yf[sl][:, :], ch_dbg)

            k.wait("tensor", ch_v, yf_done)
            for tg in range(4):
                slg = tg % 2
                for dh in range(2):
                    bank = slg * 2 + dh
                    k.wait("tensor", ch_v, po_last[bank])
                    for cc in range(DCN):
                        k.op("tensor",
                             lambda e, cc=cc, tg=tg, dh=dh, bank=bank, sl=sl:
                             e.matmul(po[bank][:, :],
                                      yf[sl][:, cc * 512 + tg * 128: cc * 512 + tg * 128 + 128],
                                      wout[:, cc * 1024 + dh * 512: cc * 1024 + (dh + 1) * 512],
                                      start=(cc == 0), stop=(cc == DCN - 1)),
                             ch_pe if cc == DCN - 1 else None)
                pov = ch_pe.n
                gidx = bi * 4 + tg
                k.wait("vector", ch_pe, pov)
                if gidx >= 2:
                    k.wait("vector", ch_a, atnLN_a.get(gidx - 2, 0))
                for dh in range(2):
                    bank = slg * 2 + dh
                    bv = k.op("vector", lambda e, dh=dh, bank=bank, slg=slg:
                              e.tensor_tensor(aot[slg][:, dh * 512:(dh + 1) * 512],
                                              po[bank][:, :], cbout[:, dh * 512:(dh + 1) * 512],
                                              ALU.add), ch_v)
                    po_last[bank] = bv
                if debug:
                    k.wait("scalar", ch_v, po_last[slg * 2 + 1])
                    k.dma("scalar", bass.AP(s_attn, (bi * 512 + tg * 128) * D, [[D, 128], [1, D]]),
                          aot[slg][:, :], ch_dbg)
                k.wait("scalar", ch_v, po_last[slg * 2 + 1])
                k.wait("vector", ch_v, po_last[slg * 2 + 1])
                lnat = _ln_emit(k, ch_v, ch_a, aot[slg][:, :], lnt[slg][:, :],
                                smC[:, slg * 8:(slg + 1) * 8], sqC[slg][:, :], EPS_LN)
                atnLN_a[gidx] = ch_a.n
                k.wait("vector", ch_a, lnat[0])
                k.wait("vector", ch_xrP[sl], xrv)
                if gidx >= 2:
                    k.wait("vector", ch_a, xinLN_a.get(gidx - 2, 0))
                    k.wait("vector", ch_x1stP[slg])
                    k.wait("vector", ch_nstP[slg])
                xv1 = k.op("vector", lambda e, slg=slg, tg=tg, sl=sl:
                           e.tensor_tensor(x1t[slg][:, :], xr[sl][:, tg * 1024:(tg + 1) * 1024],
                                           lnt[slg][:, :], ALU.add), ch_v)
                k.wait("scalar", ch_v, xv1)
                row0 = bi * 512 + tg * 128
                k.dma("scalar", bass.AP(s_x1, row0 * D, [[D, 128], [1, D]]), x1t[slg][:, :], ch_x1stP[slg])
                k.wait("vector", ch_v, xv1)
                k.wait("scalar", ch_xn2stP[slg])
                lnx = _ln_emit(k, ch_v, ch_a, x1t[slg][:, :], xint[slg][:, :],
                               smC[:, (2 + slg) * 8:(3 + slg) * 8], sqC[slg][:, :], EPS_LN)
                k.wait("scalar", ch_a, lnx[0])
                k.dma("scalar", bass.AP(s_xn2, row0 * D, [[D, 128], [1, D]]), xint[slg][:, :], ch_xn2stP[slg])
                sm = smC[:, (2 + slg) * 8:(3 + slg) * 8]
                k.wait("scalar", ch_a, lnx[0])
                a4 = k.op("scalar", lambda e, slg=slg, sm=sm:
                          e.activation(sqC[slg][:, :], xint[slg][:, :], AF.Square,
                                       accum_out=sm[:, 7:8]), ch_a)
                k.wait("vector", ch_a, a4)
                r1 = k.op("vector", lambda e, sm=sm:
                          e.tensor_scalar(sm[:, 6:7], sm[:, 7:8], 1.0 / D, EPS_RMS,
                                          ALU.mult, ALU.add), ch_v)
                k.wait("scalar", ch_v, r1)
                r2 = k.op("scalar", lambda e, sm=sm: e.activation(sm[:, 5:6], sm[:, 6:7], AF.Ln), ch_a)
                k.wait("scalar", ch_a, r2)
                r3 = k.op("scalar", lambda e, sm=sm:
                          e.activation(sm[:, 4:5], sm[:, 5:6], AF.Exp, scale=-0.5), ch_a)
                k.wait("scalar", ch_a, r3)
                r4 = k.op("scalar", lambda e, slg=slg, sm=sm:
                          e.activation(sqC[slg][:, :], xint[slg][:, :], AF.Identity,
                                       scale=sm[:, 4:5]), ch_a)
                k.wait("vector", ch_a, r4)
                r5 = k.op("vector", lambda e, slg=slg:
                          e.tensor_tensor(nrt[slg][:, :], sqC[slg][:, :], cnormw[:, :], ALU.mult), ch_v)
                k.wait("scalar", ch_v, r5)
                k.dma("scalar", bass.AP(s_normed, row0 * D, [[D, 128], [1, D]]), nrt[slg][:, :], ch_nstP[slg])
                xinLN_a[gidx] = ch_a.n
            epi_v[bi] = ch_v.n
            epi_a[bi] = ch_a.n

        k.wait("sync", ch_ringP[0])
        k.wait("sync", ch_ringP[1])
        k.wait("sync", ch_vfP[0])
        k.wait("sync", ch_vfP[1])
        k.wait("sync", ch_sfP[0])
        k.wait("sync", ch_sfP[1])
        k.wait("sync", ch_dbg)
        k.wait("scalar", ch_xrP[0])
        k.wait("scalar", ch_xrP[1])
        k.wait("scalar", ch_x1stP[0])
        k.wait("scalar", ch_x1stP[1])
        k.wait("scalar", ch_xn2stP[0])
        k.wait("scalar", ch_xn2stP[1])
        k.wait("scalar", ch_nstP[0])
        k.wait("scalar", ch_nstP[1])
        k.wait("scalar", ch_dbg)
        k.wait("gpsimd", ch_wu)
        esC.close()
        k.barrier(chb)

    ch_ntrP = [Ch(nc, "ntrA"), Ch(nc, "ntrB")]
    ch_gwP = [Ch(nc, "gwA"), Ch(nc, "gwB")]
    ch_dwP = [Ch(nc, "dwA"), Ch(nc, "dwB")]
    ch_x1rP = [Ch(nc, "x1rA"), Ch(nc, "x1rB")]
    ch_xinP = [Ch(nc, "xinA"), Ch(nc, "xinB")]

    if "D" in phases:
        esD = ExitStack()
        nrmB = [[esD.enter_context(nc.sbuf_tensor(f"nrm{p}_{i}", [128, DCN * 512], BF16))
                 for i in range(2)] for p in range(2)]
        guw = [esD.enter_context(nc.sbuf_tensor(f"guw{i}", [128, 2 * DCN * GU_SEC], BF16)) for i in range(2)]
        hid = esD.enter_context(nc.sbuf_tensor("hid", [128, HCN * TB], BF16))
        dnw = [esD.enter_context(nc.sbuf_tensor(f"dnw{i}", [128, HCN * 256], BF16)) for i in range(2)]
        accB = [esD.enter_context(nc.sbuf_tensor(f"acc{p}", [128, (TB // 128) * 1024], BF16))
                for p in range(2)]
        combB = [esD.enter_context(nc.sbuf_tensor(f"comb{p}", [128, (TB // 128) * 8], F32))
                 for p in range(2)]
        cL = esD.enter_context(nc.sbuf_tensor("cL", [128, 8], F32))
        cR = esD.enter_context(nc.sbuf_tensor("cR", [128, 8], F32))
        cE = esD.enter_context(nc.sbuf_tensor("cE", [128, 8], F32))
        smD = esD.enter_context(nc.sbuf_tensor("smD", [128, 16], F32))
        sil = [esD.enter_context(nc.sbuf_tensor(f"sil{i}", [128, 512], F32)) for i in range(3)]
        x1r = [esD.enter_context(nc.sbuf_tensor(f"x1r{i}", [128, 1024], F32)) for i in range(2)]
        xinE = [esD.enter_context(nc.sbuf_tensor(f"xinE{i}", [128, 1024], F32)) for i in range(2)]
        ffnE = esD.enter_context(nc.sbuf_tensor("ffnE", [128, 1024], F32))
        lnE = esD.enter_context(nc.sbuf_tensor("lnE", [128, 1024], F32))
        x2E = [esD.enter_context(nc.sbuf_tensor(f"x2E{i}", [128, 1024], F32)) for i in range(2)]
        sqE = esD.enter_context(nc.sbuf_tensor("sqE", [128, 1024], F32))
        smE = esD.enter_context(nc.sbuf_tensor("smE", [128, 16], F32))
        pg = [esD.enter_context(nc.psum_tensor(f"pg{i}", [128, 512], F32)) for i in range(3)]
        pup = [esD.enter_context(nc.psum_tensor(f"pup{i}", [128, 512], F32)) for i in range(2)]
        pd = [esD.enter_context(nc.psum_tensor(f"pd{i}", [128, 256], F32)) for i in range(2)]
        plog = esD.enter_context(nc.psum_tensor("plog", [128, 64], F32))

        NTG = TB // 128
        xv2s = {}
        gu_done_prev = 0
        dn_done_prev = 0
        pg_last = {0: 0, 1: 0, 2: 0}
        pup_last = {0: 0, 1: 0}
        pd_last = {0: 0, 1: 0}
        sil_last = {0: 0, 1: 0, 2: 0}
        plog_read = 0
        dance_a_prev = 0

        ntrv = [0, 0]
        gu_end = {}

        def routing_prep(tb):
            nonlocal plog_read, dance_a_prev
            p = tb % 2
            t0p = tb * TB
            k.wait("sync", ch_pe, gu_end.get(tb - 2, 0))
            for tkb in range(2):
                for dc in range(DCN):
                    k.dma("sync", nrmB[p][tkb][:, dc * 512:(dc + 1) * 512],
                          bass.AP(s_normed, (t0p + tkb * 512) * D + dc * 128, [[D, 512], [1, 128]]),
                          ch_ntrP[p], transpose=True)
            ntrv[p] = ch_ntrP[p].n
            k.wait("tensor", ch_ntrP[p], ntrv[p])
            k.wait("tensor", ch_v, plog_read)
            for tg in range(NTG):
                tkb, tgl = tg // 4, tg % 4
                for dc in range(DCN):
                    k.op("tensor",
                         lambda e, tg=tg, tkb=tkb, tgl=tgl, dc=dc, p=p:
                         e.matmul(plog[:, tg * 8:(tg + 1) * 8],
                                  nrmB[p][tkb][:, dc * 512 + tgl * 128: dc * 512 + tgl * 128 + 128],
                                  crouter[:, dc * 8:(dc + 1) * 8],
                                  start=(dc == 0), stop=(dc == DCN - 1)),
                         ch_pe if dc == DCN - 1 else None)
            lgv = ch_pe.n
            for tg in range(NTG):
                k.wait("vector", ch_pe, lgv)
                k.wait("vector", ch_a, dance_a_prev)
                d1 = k.op("vector", lambda e, tg=tg: e.tensor_copy(cL[:, :], plog[:, tg * 8:(tg + 1) * 8]), ch_v)
                k.wait("vector", ch_v, d1)
                d2 = k.op("vector", lambda e: e.reduce_max(smD[:, 0:1], cL[:, :], axis=AX.X, negate=True), ch_v)
                k.wait("scalar", ch_v, d2)
                d3 = k.op("scalar", lambda e: e.activation(cE[:, :], cL[:, :], AF.Exp, bias=smD[:, 0:1]), ch_a)
                for j in range(8):
                    dj = k.op("vector", lambda e, j=j:
                              e.tensor_single_scalar(cR[:, :], cL[:, :], cL[:, j:j + 1], ALU.is_gt), ch_v)
                    k.wait("vector", ch_v, dj)
                    dj2 = k.op("vector", lambda e, j=j:
                               e.reduce_sum(smD[:, 1 + j:2 + j], cR[:, :], axis=AX.X), ch_v)
                    k.wait("vector", ch_v, dj2)
                d4 = k.op("vector", lambda e:
                          e.tensor_single_scalar(cR[:, :], smD[:, 1:9], 1.5, ALU.is_lt), ch_v)
                k.wait("vector", ch_a, d3)
                k.wait("vector", ch_v, d4)
                d5 = k.op("vector", lambda e: e.tensor_tensor(cE[:, :], cE[:, :], cR[:, :], ALU.mult), ch_v)
                k.wait("vector", ch_v, d5)
                d6 = k.op("vector", lambda e: e.reduce_sum(smD[:, 0:1], cE[:, :], axis=AX.X), ch_v)
                k.wait("vector", ch_v, d6)
                d7 = k.op("vector", lambda e: e.reciprocal(smD[:, 1:2], smD[:, 0:1]), ch_v)
                k.wait("vector", ch_v, d7)
                k.op("vector", lambda e, tg=tg, p=p:
                     e.tensor_single_scalar(combB[p][:, tg * 8:(tg + 1) * 8], cE[:, :],
                                            smD[:, 1:2], ALU.mult), ch_v)
                dance_a_prev = d3
            plog_read = ch_v.n

        def e_chunk(tbe, tg):
            pe_ = tbe % 2
            slg = tg % 2
            gq = tbe * NTG + tg
            row0 = tbe * TB + tg * 128
            if gq >= 2:
                k.wait("sync", ch_v, xv2s[gq - 2])
                k.wait("sync", ch_v, fvs.get(gq - 2, 0))
            k.dma("sync", x1r[slg][:, :], bass.AP(s_x1, row0 * D, [[D, 128], [1, D]]), ch_x1rP[slg])
            k.dma("sync", xinE[slg][:, :], bass.AP(s_xn2, row0 * D, [[D, 128], [1, D]]), ch_xinP[slg])
            x1v = ch_x1rP[slg].n
            k.wait("vector", ch_x1rP[slg], x1v)
            k.wait("vector", ch_xinP[slg], ch_xinP[slg].n)
            fv = k.op("vector", lambda e, tg=tg, pe_=pe_, slg=slg:
                      e.scalar_tensor_tensor(ffnE[:, :], accB[pe_][:, tg * 1024:(tg + 1) * 1024],
                                             calpha[:, 0:1], xinE[slg][:, :], ALU.mult, ALU.add), ch_v)
            fvs[gq] = fv
            k.wait("vector", ch_v, fv)
            k.wait("scalar", ch_v, fv)
            lnret2 = _ln_emit(k, ch_v, ch_a, ffnE[:, :], lnE[:, :], smE[:, 8:16], sqE[:, :], EPS_LN)
            k.wait("vector", ch_a, lnret2[0])
            if gq >= 2:
                k.wait("vector", ch_outP[slg])
            xv2 = k.op("vector", lambda e, slg=slg:
                       e.tensor_tensor(x2E[slg][:, :], x1r[slg][:, :], lnE[:, :], ALU.add), ch_v)
            xv2s[gq] = xv2
            k.wait("scalar", ch_v, xv2)
            k.dma("scalar", bass.AP(y_out, row0 * D, [[D, 128], [1, D]]), x2E[slg][:, :], ch_outP[slg])

        gw_pending = {0: 0, 1: 0}
        fvs = {}

        def issue_gu_sections(en, s, war_val):
            slw = s % 2
            k.wait("gpsimd", ch_pe, war_val)
            for side in range(2):
                hoff = side * HID + s * GU_SEC
                k.dma("gpsimd", guw[slw][:, side * DCN * GU_SEC:(side + 1) * DCN * GU_SEC],
                      bass.AP(gu_T, en * D * 2 * HID + hoff,
                              [[2 * HID, 128], [128 * 2 * HID, DCN], [1, GU_SEC]]),
                      ch_gwP[slw])
            return ch_gwP[slw].n

        gw_pending[0] = issue_gu_sections(0, 0, 0)
        gw_pending[1] = issue_gu_sections(0, 1, 0)
        routing_prep(0)
        for tb in range(ntb):
            p = tb % 2
            nrm = nrmB[p]
            comb = combB[p]
            acc = accB[p]
            for e_ in range(nexp):
                if d_stage < 2:
                    continue
                hse = 0
                sec_end = {}
                for s in range(NSEC):
                    slw = s % 2
                    if s >= 2:
                        gw_pending[slw] = issue_gu_sections(e_, s, sec_end[s - 2])
                    gwv = gw_pending[slw]
                    k.wait("tensor", ch_gwP[slw], gwv)
                    for tkb in range(2):
                        for hcl in range(2):
                            hc = s * 2 + hcl
                            par = hse % 2
                            pr3 = hse % 3
                            hse += 1
                            k.wait("tensor", ch_a, pg_last[pr3])
                            for dc in range(DCN):
                                k.op("tensor",
                                     lambda e, dc=dc, hcl=hcl, tkb=tkb, par=pr3, slw=slw, p=p:
                                     e.matmul(pg[par][:, :],
                                              guw[slw][:, dc * GU_SEC + hcl * 128: dc * GU_SEC + hcl * 128 + 128],
                                              nrmB[p][tkb][:, dc * 512:(dc + 1) * 512],
                                              start=(dc == 0), stop=(dc == DCN - 1)),
                                     ch_pe if dc == DCN - 1 else None)
                            k.wait("tensor", ch_v, pup_last[par])
                            for dc in range(DCN):
                                k.op("tensor",
                                     lambda e, dc=dc, hcl=hcl, tkb=tkb, par=par, slw=slw, p=p:
                                     e.matmul(pup[par][:, :],
                                              guw[slw][:, DCN * GU_SEC + dc * GU_SEC + hcl * 128:
                                                       DCN * GU_SEC + dc * GU_SEC + hcl * 128 + 128],
                                              nrmB[p][tkb][:, dc * 512:(dc + 1) * 512],
                                              start=(dc == 0), stop=(dc == DCN - 1)),
                                     ch_pe if dc == DCN - 1 else None)
                            guv = ch_pe.n
                            gu_sec_done = guv
                            k.wait("scalar", ch_pe, guv)
                            k.wait("scalar", ch_v, sil_last[pr3])
                            av = k.op("scalar", lambda e, pr3=pr3:
                                      e.activation(sil[pr3][:, :], pg[pr3][:, :], AF.Silu), ch_a)
                            pg_last[pr3] = av
                            k.wait("vector", ch_a, av)
                            k.wait("vector", ch_pe, dn_done_prev)
                            hv = k.op("vector", lambda e, par=par, pr3=pr3, hc=hc, tkb=tkb:
                                      e.tensor_tensor(hid[:, hc * TB + tkb * 512: hc * TB + tkb * 512 + 512],
                                                      sil[pr3][:, :], pup[par][:, :], ALU.mult), ch_v)
                            pup_last[par] = hv
                            sil_last[pr3] = hv
                    sec_end[s] = ch_pe.n
                gu_done_prev = ch_pe.n
                hid_done = ch_v.n
                if e_ == nexp - 1:
                    gu_end[tb] = gu_done_prev

                if d_stage < 3:
                    continue
                dn_q_end = {}
                for q in range(4):
                    slq = q % 2
                    if q == 2 and (e_ + 1 < nexp or tb + 1 < ntb):
                        en = (e_ + 1) % nexp
                        gw_pending[0] = issue_gu_sections(en, 0, gu_done_prev)
                        gw_pending[1] = issue_gu_sections(en, 1, gu_done_prev)
                    k.wait("gpsimd", ch_pe, dn_q_end[q - 2] if q >= 2 else dn_done_prev)
                    k.dma("gpsimd", dnw[slq][:, :],
                          bass.AP(dn_T, e_ * HID * D + q * 256,
                                  [[D, 128], [128 * D, HCN], [1, 256]]),
                          ch_dwP[slq])
                    dwv = ch_dwP[slq].n
                    k.wait("tensor", ch_dwP[slq], dwv)
                    k.wait("tensor", ch_v, hid_done)
                    for tg in range(NTG):
                        par = (q * NTG + tg) % 2
                        k.wait("tensor", ch_v, pd_last[par])
                        for hc in range(HCN):
                            k.op("tensor",
                                 lambda e, hc=hc, tg=tg, par=par, slq=slq:
                                 e.matmul(pd[par][:, :],
                                          hid[:, hc * TB + tg * 128: hc * TB + tg * 128 + 128],
                                          dnw[slq][:, hc * 256:(hc + 1) * 256],
                                          start=(hc == 0), stop=(hc == HCN - 1)),
                                 ch_pe if hc == HCN - 1 else None)
                        dn_q_done = ch_pe.n
                        k.wait("vector", ch_pe, dn_q_done)
                        accv = acc[:, tg * 1024 + q * 256: tg * 1024 + (q + 1) * 256]
                        cc_ = comb[:, tg * 8 + e_: tg * 8 + e_ + 1]
                        if e_ == 0:
                            uv = k.op("vector", lambda e, accv=accv, cc_=cc_, par=par:
                                      e.tensor_single_scalar(accv, pd[par][:, :], cc_, ALU.mult), ch_v)
                        else:
                            uv = k.op("vector", lambda e, accv=accv, cc_=cc_, par=par:
                                      e.scalar_tensor_tensor(accv, pd[par][:, :], cc_, accv,
                                                             ALU.mult, ALU.add), ch_v)
                        pd_last[par] = uv
                    dn_q_end[q] = ch_pe.n
                dn_done_prev = ch_pe.n

                if d_stage >= 4:
                    if e_ == 0 and tb + 1 < ntb:
                        routing_prep(tb + 1)
                    if tb > 0:
                        e_chunk(tb - 1, e_)

            if d_stage >= 4 and tb == ntb - 1:
                for tg in range(NTG):
                    e_chunk(tb, tg)

        k.wait("scalar", ch_outP[0])
        k.wait("scalar", ch_outP[1])
        k.wait("sync", ch_ntrP[0])
        k.wait("sync", ch_ntrP[1])
        k.wait("sync", ch_x1rP[0])
        k.wait("sync", ch_x1rP[1])
        k.wait("sync", ch_xinP[0])
        k.wait("sync", ch_xinP[1])
        k.wait("gpsimd", ch_gwP[0])
        k.wait("gpsimd", ch_gwP[1])
        k.wait("gpsimd", ch_dwP[0])
        k.wait("gpsimd", ch_dwP[1])
        esD.close()
    else:
        esY = ExitStack()
        yz = esY.enter_context(nc.sbuf_tensor("yz", [128, 1024], F32))
        vz2 = k.op("vector", lambda e: e.memset(yz[:, :], 0.0), ch_v)
        k.wait("scalar", ch_v, vz2)
        for r in range(NOWN // 128):
            k.dma("scalar", bass.AP(y_out, r * 128 * D, [[D, 128], [1, D]]), yz[:, :], ch_outP[0])
        k.wait("scalar", ch_outP[0])
        esY.close()

    k.replay()
    es_res.close()
    return nc


def _host_prep(inputs):
    x = np.asarray(inputs["L0_attn_x"], dtype=np.float32)
    w_in = np.asarray(inputs["L0_attn_w_in"], np.float32)
    b_in = np.asarray(inputs["L0_attn_b_in"], np.float32)
    w_conv = np.asarray(inputs["L0_attn_w_conv"], np.float32)
    b_conv = np.asarray(inputs["L0_attn_b_conv"], np.float32)
    b1 = np.asarray(inputs["L0_attn_b1"], np.float32)
    w2 = np.asarray(inputs["L0_attn_w2"], np.float32)
    b2 = np.asarray(inputs["L0_attn_b2"], np.float32)
    w3 = np.asarray(inputs["L0_attn_w3"], np.float32)
    fb = np.asarray(inputs["L0_attn_filt_bias"], np.float32)
    w_out = np.asarray(inputs["L0_attn_w_out"], np.float32)
    b_out = np.asarray(inputs["L0_attn_b_out"], np.float32)
    norm_w = np.asarray(inputs["L0_ffn_norm_weight"], np.float32)
    router = np.asarray(inputs["L0_ffn_router_weight"], np.float32)
    gu = np.asarray(inputs["L0_ffn_gate_up_weight"], np.float32)
    dn = np.asarray(inputs["L0_ffn_down_weight"], np.float32)
    alpha = np.asarray(inputs["L0_ffn_residual_alpha"], np.float32).reshape(1, 1)

    sel = np.concatenate([np.arange(0, 1024), np.arange(2048, 3072)])
    w_in_sel = np.ascontiguousarray(w_in[:, sel])
    b_in_sel = np.ascontiguousarray(b_in[sel].reshape(ECN, 128).T)
    wc = w_conv[:, 0, :][sel]
    wconv_sel = np.ascontiguousarray(wc.reshape(ECN, 128, 3).transpose(1, 0, 2).reshape(128, ECN * 3))
    b_conv_sel = np.ascontiguousarray(b_conv[sel].reshape(ECN, 128).T)
    fb_pc = np.ascontiguousarray(fb.reshape(8, 128).T)
    router_Tn = np.ascontiguousarray(router.T)
    gu_Tn = np.ascontiguousarray(gu.transpose(0, 2, 1))
    dn_Tn = np.ascontiguousarray(dn.transpose(0, 2, 1))
    rev = np.zeros((128, 128), np.float32)
    rev[np.arange(128), 127 - np.arange(128)] = 1.0
    diag = np.zeros((ECN * 3, 128, 128), np.float32)
    ii = np.arange(128)
    for ec in range(ECN):
        for kk in range(3):
            diag[ec * 3 + kk, ii, ii] = wc[ec * 128 + ii, kk]

    shared = {
        "w_in_sel": w_in_sel, "b_in_sel": b_in_sel, "wconv_sel": wconv_sel,
        "b_conv_sel": b_conv_sel, "b1c": np.ascontiguousarray(b1.reshape(64, 1)),
        "b2c": np.ascontiguousarray(b2.reshape(64, 1)), "w2i": w2, "w3i": w3,
        "fb_pc": fb_pc, "w_out_i": w_out, "b_out_i": b_out,
        "norm_w_i": norm_w, "router_T": router_Tn, "gu_T": gu_Tn, "dn_T": dn_Tn,
        "alpha_i": alpha, "rev_i": rev, "diagw": diag,
    }
    in_maps = []
    for c in range(8):
        bb, half = c // 2, c % 2
        xsn = np.zeros((NPOS, D), np.float32)
        if half == 0:
            xsn[HALO:] = x[bb, 0:NOWN]
            m = np.zeros(NPOS, np.float32)
            m[HALO:] = 1.0
        else:
            xsn[:, :] = x[bb, NOWN - HALO: 2 * NOWN]
            m = np.ones(NPOS, np.float32)
        pm = np.ascontiguousarray(m.reshape(NPOS // 128, 128).T)
        d = {"xs": xsn, "pmask": pm,
             "flag_i": np.full((1, 1), 1.0 if half == 0 else 0.0, np.float32)}
        d.update(shared)
        in_maps.append(d)
    return in_maps


def kernel(**inputs):
    nc = build_nc()
    in_maps = _host_prep(inputs)
    res = run_bass_kernel_spmd(nc, in_maps, core_ids=list(range(8)))
    out = np.empty((4, 8192, 1024), np.float32)
    for c in range(8):
        bb, half = c // 2, c % 2
        out[bb, half * NOWN:(half + 1) * NOWN] = np.asarray(res.results[c]["y_out"])
    return out


# version 10
# speedup vs baseline: 1.1328x; 1.1328x over previous
import numpy as np
from contextlib import ExitStack

import concourse.bass as bass
import concourse.mybir as mybir
from concourse.bass_utils import run_bass_kernel_spmd

F32 = mybir.dt.float32
BF16 = mybir.dt.bfloat16
AF = mybir.ActivationFunctionType
ALU = mybir.AluOpType
AX = mybir.AxisListType

NPOS = 5632
HALO = 1536
NOWN = 4096
NB = 11
NCB = 8
D = 1024
DCN = 8
ECN = 16
HID = 2816
HCN = 22
GU_SEC = 256
NSEC = HID // GU_SEC
TB = 1024
NTB = NOWN // TB
NEXP = 8
EPS_LN = 1e-5
EPS_RMS = 1e-6

ENGINES = ("sync", "scalar", "vector", "tensor", "gpsimd")


class Ch:
    __slots__ = ("sem", "n")

    def __init__(self, nc, name):
        self.sem = nc.alloc_semaphore(name)
        self.n = 0


class K:
    def __init__(self, nc):
        self.nc = nc
        self.st = {e: [] for e in ENGINES}

    def raw(self, eng, f):
        self.st[eng].append(f)

    def wait(self, eng, ch, val=None):
        v = ch.n if val is None else val
        if v > 0:
            self.raw(eng, lambda e, s=ch.sem, v=v: e.wait_ge(s, v))

    def op(self, eng, f, inc=None, k=1):
        if inc is None:
            self.raw(eng, f)
            return None
        inc.n += k
        self.raw(eng, lambda e, f=f, s=inc.sem, k=k: f(e).then_inc(s, k))
        return inc.n

    def dma(self, eng, out, in_, inc, **kw):
        inc.n += 16
        self.raw(eng, lambda e, o=out, i=in_, s=inc.sem, kw=dict(kw):
                 e.dma_start(out=o, in_=i, **kw).then_inc(s, 16))
        return inc.n

    def barrier(self, chb):
        chb.n += len(ENGINES)
        tgt = chb.n
        for eng in ENGINES:
            self.raw(eng, lambda e, s=chb.sem: e.sem_inc(s, 1))
            self.raw(eng, lambda e, s=chb.sem, v=tgt: e.wait_ge(s, v))

    def replay(self):
        nc = self.nc
        with nc.Block() as block:
            @block.sync
            def _(eng):
                for f in self.st["sync"]:
                    f(eng)

            @block.scalar
            def _(eng):
                for f in self.st["scalar"]:
                    f(eng)

            @block.vector
            def _(eng):
                for f in self.st["vector"]:
                    f(eng)

            @block.tensor
            def _(eng):
                for f in self.st["tensor"]:
                    f(eng)

            @block.gpsimd
            def _(eng):
                for f in self.st["gpsimd"]:
                    f(eng)


def _ln_emit(k, ch_v, ch_a, src_view, dst_view, sm, sq_scr, eps, scale_tile=None):
    nsum = sm[:, 0:1]
    ssq = sm[:, 1:2]
    var = sm[:, 2:3]
    msq = sm[:, 3:4]
    rstd = sm[:, 4:5]
    biasv = sm[:, 5:6]
    mneg = sm[:, 6:7]
    vr = k.op("vector", lambda e: e.reduce_sum(nsum, src_view, axis=AX.X, negate=True), ch_v)
    asq = k.op("scalar", lambda e: e.activation(sq_scr, src_view, AF.Square, accum_out=ssq), ch_a)
    k.wait("vector", ch_a, asq)
    v1 = k.op("vector", lambda e: e.tensor_scalar(var, ssq, 1.0 / D, float(eps), ALU.mult, ALU.add), ch_v)
    k.wait("vector", ch_v, vr)
    v2 = k.op("vector", lambda e: e.tensor_scalar(mneg, nsum, 1.0 / D, None, ALU.mult), ch_v)
    k.wait("vector", ch_v, v2)
    v3 = k.op("vector", lambda e: e.tensor_tensor(msq, mneg, mneg, ALU.mult), ch_v)
    k.wait("vector", ch_v, v3)
    v4 = k.op("vector", lambda e: e.tensor_tensor(var, var, msq, ALU.subtract), ch_v)
    k.wait("scalar", ch_v, v4)
    a1 = k.op("scalar", lambda e: e.activation(rstd, var, AF.Ln), ch_a)
    k.wait("scalar", ch_a, a1)
    a2 = k.op("scalar", lambda e: e.activation(rstd, rstd, AF.Exp, scale=-0.5), ch_a)
    k.wait("vector", ch_a, a2)
    v5 = k.op("vector", lambda e: e.tensor_tensor(biasv, mneg, rstd, ALU.mult), ch_v)
    k.wait("scalar", ch_v, v5)
    k.wait("scalar", ch_a, a2)
    if scale_tile is None:
        a3 = k.op("scalar", lambda e: e.activation(dst_view, src_view, AF.Identity, bias=biasv, scale=rstd), ch_a)
        return a3, v5
    a3 = k.op("scalar", lambda e: e.activation(sq_scr, src_view, AF.Identity, bias=biasv, scale=rstd), ch_a)
    k.wait("vector", ch_a, a3)
    v6 = k.op("vector", lambda e: e.tensor_tensor(dst_view, sq_scr, scale_tile, ALU.mult), ch_v)
    return a3, v6


def _ln_emit4(k, ch_v, ch_a, srcs, dsts, sm, sqs, eps):
    n = len(srcs)
    nsum = sm[:, 0:n]
    ssq = sm[:, 4:4 + n]
    var = sm[:, 8:8 + n]
    mneg = sm[:, 12:12 + n]
    msq = sm[:, 16:16 + n]
    rstd = sm[:, 20:20 + n]
    biasv = sm[:, 24:24 + n]
    vr = 0
    for g in range(n):
        vr = k.op("vector", lambda e, g=g: e.reduce_sum(sm[:, g:g + 1], srcs[g], axis=AX.X, negate=True), ch_v)
    asq = 0
    for g in range(n):
        asq = k.op("scalar", lambda e, g=g: e.activation(sqs[g], srcs[g], AF.Square,
                                                         accum_out=sm[:, 4 + g:5 + g]), ch_a)
    k.wait("vector", ch_a, asq)
    v1 = k.op("vector", lambda e: e.tensor_scalar(var, ssq, 1.0 / D, float(eps), ALU.mult, ALU.add), ch_v)
    k.wait("vector", ch_v, vr)
    v2 = k.op("vector", lambda e: e.tensor_scalar(mneg, nsum, 1.0 / D, None, ALU.mult), ch_v)
    k.wait("vector", ch_v, v2)
    v3 = k.op("vector", lambda e: e.tensor_tensor(msq, mneg, mneg, ALU.mult), ch_v)
    k.wait("vector", ch_v, v3)
    v4 = k.op("vector", lambda e: e.tensor_tensor(var, var, msq, ALU.subtract), ch_v)
    k.wait("scalar", ch_v, v4)
    a1 = k.op("scalar", lambda e: e.activation(rstd, var, AF.Ln), ch_a)
    k.wait("scalar", ch_a, a1)
    a2 = k.op("scalar", lambda e: e.activation(rstd, rstd, AF.Exp, scale=-0.5), ch_a)
    k.wait("vector", ch_a, a2)
    v5 = k.op("vector", lambda e: e.tensor_tensor(biasv, mneg, rstd, ALU.mult), ch_v)
    k.wait("scalar", ch_v, v5)
    a3 = 0
    for g in range(n):
        a3 = k.op("scalar", lambda e, g=g: e.activation(dsts[g], srcs[g], AF.Identity,
                                                        bias=sm[:, 24 + g:25 + g],
                                                        scale=sm[:, 20 + g:21 + g]), ch_a)
    return a3


def build_nc(debug=False, phases=("0", "AB", "C", "D"), ab_stage=5, nb=NB,
             ntb=NTB, nexp=NEXP, d_stage=4):
    nc = bass.Bass("TRN2", target_bir_lowering=False)

    xs = nc.dram_tensor("xs", [NPOS, D], F32, kind="ExternalInput")
    pmask = nc.dram_tensor("pmask", [128, NPOS // 128], F32, kind="ExternalInput")
    w_in_sel = nc.dram_tensor("w_in_sel", [D, 2048], F32, kind="ExternalInput")
    b_in_sel = nc.dram_tensor("b_in_sel", [128, ECN], F32, kind="ExternalInput")
    wconv_sel = nc.dram_tensor("wconv_sel", [128, ECN * 3], F32, kind="ExternalInput")
    b_conv_sel = nc.dram_tensor("b_conv_sel", [128, ECN], F32, kind="ExternalInput")
    b1c = nc.dram_tensor("b1c", [64, 1], F32, kind="ExternalInput")
    b2c = nc.dram_tensor("b2c", [64, 1], F32, kind="ExternalInput")
    w2i = nc.dram_tensor("w2i", [64, 64], F32, kind="ExternalInput")
    w3i = nc.dram_tensor("w3i", [64, D], F32, kind="ExternalInput")
    fb_pc = nc.dram_tensor("fb_pc", [128, 8], F32, kind="ExternalInput")
    w_out_i = nc.dram_tensor("w_out_i", [D, D], F32, kind="ExternalInput")
    b_out_i = nc.dram_tensor("b_out_i", [D], F32, kind="ExternalInput")
    norm_w_i = nc.dram_tensor("norm_w_i", [D], F32, kind="ExternalInput")
    router_T = nc.dram_tensor("router_T", [D, 8], F32, kind="ExternalInput")
    gu_T = nc.dram_tensor("gu_T", [NEXP, D, 2 * HID], F32, kind="ExternalInput")
    dn_T = nc.dram_tensor("dn_T", [NEXP, HID, D], F32, kind="ExternalInput")
    alpha_i = nc.dram_tensor("alpha_i", [1, 1], F32, kind="ExternalInput")
    rev_i = nc.dram_tensor("rev_i", [128, 128], F32, kind="ExternalInput")
    diagw = nc.dram_tensor("diagw", [ECN * 3, 128, 128], F32, kind="ExternalInput")
    flag_i = nc.dram_tensor("flag_i", [1, 1], F32, kind="ExternalInput")

    y_out = nc.dram_tensor("y_out", [NOWN, D], F32, kind="ExternalOutput")

    skind = "ExternalOutput" if debug else "Internal"
    s_hpad = nc.dram_tensor("s_hpad", [2176], F32, kind=skind)
    s_v1 = nc.dram_tensor("s_v1", [D, NPOS], BF16, kind=skind)
    s_s0 = nc.dram_tensor("s_s0", [D, NPOS], BF16, kind=skind)
    s_x1 = nc.dram_tensor("s_x1", [NOWN, D], BF16, kind=skind)
    s_xn2 = nc.dram_tensor("s_xn2", [NOWN, D], BF16, kind=skind)
    s_normed = nc.dram_tensor("s_normed", [NOWN, D], BF16, kind=skind)
    if debug:
        s_y = nc.dram_tensor("s_y", [D, NOWN], BF16, kind="ExternalOutput")
        s_attn = nc.dram_tensor("s_attn", [NOWN, D], F32, kind="ExternalOutput")

    k = K(nc)

    chb = Ch(nc, "bar")
    ch_v = Ch(nc, "v")
    ch_a = Ch(nc, "a")
    ch_p = Ch(nc, "p")
    ch_pe = Ch(nc, "pe")
    ch_c0 = Ch(nc, "c0")
    ch_wu = Ch(nc, "wu")
    ch_xP = [Ch(nc, "xA"), Ch(nc, "xB")]
    ch_v1stP = [Ch(nc, "v1stA"), Ch(nc, "v1stB")]
    ch_s0stP = [Ch(nc, "s0stA"), Ch(nc, "s0stB")]
    ch_hp = Ch(nc, "hp")

    es_res = ExitStack()

    G = es_res.enter_context(nc.sbuf_tensor("G", [128, 2048], BF16))
    cb_in = es_res.enter_context(nc.sbuf_tensor("cb_in", [128, ECN], F32))
    cwconv = es_res.enter_context(nc.sbuf_tensor("cwconv", [128, ECN * 3], F32))
    cb_conv = es_res.enter_context(nc.sbuf_tensor("cb_conv", [128, ECN], F32))
    cfb = es_res.enter_context(nc.sbuf_tensor("cfb", [128, 8], F32))
    cpm = es_res.enter_context(nc.sbuf_tensor("cpm", [128, NPOS // 128], F32))
    cbout = es_res.enter_context(nc.sbuf_tensor("cbout", [128, D], F32))
    cnormw = es_res.enter_context(nc.sbuf_tensor("cnormw", [128, D], F32))
    calpha = es_res.enter_context(nc.sbuf_tensor("calpha", [128, 1], F32))
    crouter = es_res.enter_context(nc.sbuf_tensor("crouter", [128, DCN * 8], BF16))
    cflag = es_res.enter_context(nc.sbuf_tensor("cflag", [128, 1], F32))

    k.dma("sync", cb_in[:, :], b_in_sel.ap(), ch_c0)
    k.dma("sync", cwconv[:, :], wconv_sel.ap(), ch_c0)
    k.dma("sync", cb_conv[:, :], b_conv_sel.ap(), ch_c0)
    k.dma("sync", cfb[:, :], fb_pc.ap(), ch_c0)
    k.dma("sync", cpm[:, :], pmask.ap(), ch_c0)
    k.dma("sync", cbout[:, :], bass.AP(b_out_i, 0, [[0, 128], [1, D]]), ch_c0)
    k.dma("sync", cnormw[:, :], bass.AP(norm_w_i, 0, [[0, 128], [1, D]]), ch_c0)
    k.dma("sync", calpha[:, :], bass.AP(alpha_i, 0, [[0, 128], [1, 1]]), ch_c0)
    k.dma("sync", cflag[:, :], bass.AP(flag_i, 0, [[0, 128], [1, 1]]), ch_c0)
    k.dma("gpsimd", crouter[:, :], bass.AP(router_T, 0, [[8, 128], [1024, DCN], [1, 8]]), ch_wu)

    es0 = ExitStack()
    t_b1 = es0.enter_context(nc.sbuf_tensor("t_b1", [64, 1], F32))
    t_b2 = es0.enter_context(nc.sbuf_tensor("t_b2", [64, 1], F32))
    t_w2 = es0.enter_context(nc.sbuf_tensor("t_w2", [64, 64], F32))
    t_w3 = es0.enter_context(nc.sbuf_tensor("t_w3", [64, D], F32))
    t_rev = es0.enter_context(nc.sbuf_tensor("t_rev", [128, 128], F32))
    t_s1 = es0.enter_context(nc.sbuf_tensor("t_s1", [64, 1], F32))
    t_a = es0.enter_context(nc.sbuf_tensor("t_a", [64, 1], F32))
    t_hb = es0.enter_context(nc.sbuf_tensor("t_hb", [1, 2176], F32))
    t_hank = es0.enter_context(nc.sbuf_tensor("t_hank", [128, 2048], F32))
    p_h1 = es0.enter_context(nc.psum_tensor("p_h1", [64, 1], F32))
    p_h2 = es0.enter_context(nc.psum_tensor("p_h2", [1, D], F32))
    p_fl = es0.enter_context(nc.psum_tensor("p_fl", [128, 512], F32))

    k.dma("sync", t_b1[:, :], b1c.ap(), ch_c0)
    k.dma("sync", t_b2[:, :], b2c.ap(), ch_c0)
    k.dma("sync", t_w2[:, :], w2i.ap(), ch_c0)
    k.dma("sync", t_w3[:, :], w3i.ap(), ch_c0)
    k.dma("sync", t_rev[:, :], rev_i.ap(), ch_c0)

    k.wait("scalar", ch_c0)
    a1 = k.op("scalar", lambda e: e.activation(t_s1[:, :], t_b1[:, :], AF.Silu), ch_a)
    k.wait("tensor", ch_a, a1)
    k.wait("tensor", ch_c0)
    m1 = k.op("tensor", lambda e: e.matmul(p_h1[:, :], t_w2[:, :], t_s1[:, :], start=True, stop=True), ch_pe)
    k.wait("vector", ch_pe, m1)
    k.wait("vector", ch_c0)
    v1_ = k.op("vector", lambda e: e.tensor_add(t_a[:, :], p_h1[:, :], t_b2[:, :]), ch_v)
    k.wait("scalar", ch_v, v1_)
    a2 = k.op("scalar", lambda e: e.activation(t_a[:, :], t_a[:, :], AF.Silu), ch_a)
    vz = k.op("vector", lambda e: e.memset(t_hb[:, :], 0.0), ch_v)
    k.wait("tensor", ch_a, a2)
    k.op("tensor", lambda e: e.matmul(p_h2[:, 0:512], t_a[:, :], t_w3[:, 0:512], start=True, stop=True), ch_pe)
    m3 = k.op("tensor", lambda e: e.matmul(p_h2[:, 512:1024], t_a[:, :], t_w3[:, 512:1024], start=True, stop=True), ch_pe)
    k.wait("vector", ch_pe, m3)
    k.wait("vector", ch_v, vz)
    vh = k.op("vector", lambda e: e.tensor_copy(t_hb[:, 512:1536], p_h2[:, :]), ch_v)
    k.wait("sync", ch_v, vh)
    k.dma("sync", s_hpad.ap(), t_hb[:, :], ch_hp)
    k.wait("sync", ch_hp)
    for t0 in range(0, 2048, 512):
        k.dma("sync", t_hank[:, t0:t0 + 512], bass.AP(s_hpad, 1 + t0, [[1, 128], [1, 512]]), ch_hp)
    k.wait("tensor", ch_hp)
    for t0 in range(0, 2048, 512):
        mf = k.op("tensor", lambda e, t0=t0: e.matmul(p_fl[:, :], t_rev[:, :], t_hank[:, t0:t0 + 512],
                                                      start=True, stop=True), ch_pe)
        k.wait("scalar", ch_pe, mf)
        ag = k.op("scalar", lambda e, t0=t0: e.activation(G[:, t0:t0 + 512], p_fl[:, :], AF.Copy), ch_a)
        if t0 + 512 < 2048:
            k.wait("tensor", ch_a, ag)

    k.wait("sync", ch_c0)
    k.wait("sync", ch_hp)
    es0.close()
    k.barrier(chb)

    if "AB" in phases:
        esAB = ExitStack()
        win = esAB.enter_context(nc.sbuf_tensor("win", [128, DCN * 2048], BF16))
        cdiag = esAB.enter_context(nc.sbuf_tensor("cdiag", [128, ECN * 3 * 128], BF16))
        ubias = esAB.enter_context(nc.sbuf_tensor("ubias", [128, ECN], F32))
        wsum = esAB.enter_context(nc.sbuf_tensor("wsum", [128, ECN], F32))
        xa = [esAB.enter_context(nc.sbuf_tensor(f"xa{i}", [128, 4096], F32)) for i in range(2)]
        lnb = [esAB.enter_context(nc.sbuf_tensor(f"lnb{i}", [128, 4096], BF16)) for i in range(2)]
        lnf = [esAB.enter_context(nc.sbuf_tensor(f"lnf{i}", [128, DCN * 512], BF16)) for i in range(2)]
        zt = esAB.enter_context(nc.sbuf_tensor("zt", [128, ECN * 516], BF16))
        u0b = [esAB.enter_context(nc.sbuf_tensor(f"u0b{i}", [128, 8 * 512], BF16)) for i in range(2)]
        u2t = esAB.enter_context(nc.sbuf_tensor("u2t", [128, 512], BF16))
        v1b = [esAB.enter_context(nc.sbuf_tensor(f"v1b{i}", [128, 8 * 512], BF16)) for i in range(2)]
        smAB = esAB.enter_context(nc.sbuf_tensor("smAB", [128, 4 * 8], F32))
        sqAB = esAB.enter_context(nc.sbuf_tensor("sqAB", [128, 4096], F32))
        ident = esAB.enter_context(nc.sbuf_tensor("ident", [128, 128], BF16))
        pz = [esAB.enter_context(nc.psum_tensor(f"pz{i}", [128, 512], F32)) for i in range(2)]
        pu = [esAB.enter_context(nc.psum_tensor(f"pu{i}", [128, 512], F32)) for i in range(2)]
        ptr = [esAB.enter_context(nc.psum_tensor(f"ptr{i}", [128, 512], BF16)) for i in range(2)]

        for dc in range(DCN):
            k.dma("gpsimd", win[:, dc * 2048:(dc + 1) * 2048],
                  w_in_sel[dc * 128:(dc + 1) * 128, :], ch_wu)
        k.dma("gpsimd", cdiag[:, :], bass.AP(diagw, 0, [[128, 128], [16384, ECN * 3], [1, 128]]), ch_wu)
        wu_win = ch_wu.n

        k.wait("vector", ch_c0)
        s1v = k.op("vector", lambda e: e.tensor_tensor(
            wsum[:, :], bass.AP(cwconv, 0, [[ECN * 3, 128], [3, ECN]]),
            bass.AP(cwconv, 1, [[ECN * 3, 128], [3, ECN]]), ALU.add), ch_v)
        k.wait("vector", ch_v, s1v)
        s2v = k.op("vector", lambda e: e.tensor_tensor(
            wsum[:, :], wsum[:, :], bass.AP(cwconv, 2, [[ECN * 3, 128], [3, ECN]]), ALU.add), ch_v)
        k.wait("vector", ch_v, s2v)
        s3v = k.op("vector", lambda e: e.tensor_tensor(wsum[:, :], wsum[:, :], cb_in[:, :], ALU.mult), ch_v)
        k.wait("vector", ch_v, s3v)
        ubv = k.op("vector", lambda e: e.tensor_tensor(ubias[:, :], wsum[:, :], cb_conv[:, :], ALU.add), ch_v)
        deln = esAB.enter_context(nc.sbuf_tensor("deln", [128, 2 * ECN], F32))
        k.wait("vector", ch_v, ubv)
        f1 = k.op("vector", lambda e: e.tensor_tensor(
            deln[:, 0:ECN], bass.AP(cwconv, 0, [[ECN * 3, 128], [3, ECN]]),
            bass.AP(cwconv, 1, [[ECN * 3, 128], [3, ECN]]), ALU.add), ch_v)
        k.wait("vector", ch_v, f1)
        f2 = k.op("vector", lambda e: e.tensor_tensor(deln[:, 0:ECN], deln[:, 0:ECN], cb_in[:, :], ALU.mult), ch_v)
        k.wait("vector", ch_v, f2)
        f3 = k.op("vector", lambda e: e.tensor_scalar(deln[:, 0:ECN], deln[:, 0:ECN], -1.0, None, ALU.mult), ch_v)
        k.wait("vector", ch_v, f3)
        f4 = k.op("vector", lambda e: e.tensor_tensor(
            deln[:, ECN:2 * ECN], bass.AP(cwconv, 0, [[ECN * 3, 128], [3, ECN]]), cb_in[:, :], ALU.mult), ch_v)
        k.wait("vector", ch_v, f4)
        f5 = k.op("vector", lambda e: e.tensor_scalar(deln[:, ECN:2 * ECN], deln[:, ECN:2 * ECN], -1.0, None, ALU.mult), ch_v)
        k.wait("vector", ch_v, f5)
        k.wait("vector", ch_c0)
        f6 = k.op("vector", lambda e: e.tensor_single_scalar(deln[:, :], deln[:, :], cflag[:, 0:1], ALU.mult), ch_v)

        pz0 = k.op("gpsimd", lambda e: e.memset(zt[:, :], 0.0), ch_p)
        idm = k.op("gpsimd", lambda e: e.memset(ident[:, :], 0.0), ch_p)
        k.wait("gpsimd", ch_p, idm)
        idv = k.op("gpsimd", lambda e: e.affine_select(
            out=ident[:, :], in_=ident[:, :], compare_op=ALU.not_equal,
            fill=1.0, base=0, pattern=[[-1, 128]], channel_multiplier=1), ch_p)
        k.wait("tensor", ch_p, idv)

        zev = {}
        uev = {}
        carry = {}
        tapmm = {}
        v1mul = {}
        ab_act_done = [0] * NB
        ab_z_done = [0] * NB
        ab_taps_v = [0] * NB

        k.wait("tensor", ch_wu, wu_win)
        k.wait("scalar", ch_p, pz0)
        k.wait("scalar", ch_v, ubv)

        pz_last = {0: 0, 1: 0}
        pu_last = {0: 0, 1: 0}
        ptr_last = {0: 0, 1: 0}
        ab_tr_done = [0] * NB

        for b in range(nb):
            sl = b % 2
            if b >= 2:
                k.wait("sync", ch_a, ab_act_done[b - 2])
            k.dma("sync", xa[sl][:, :],
                  bass.AP(xs, b * 512 * D, [[D, 128], [128 * D, 4], [1, D]]), ch_xP[sl])
            xv = ch_xP[sl].n
            if b >= 2:
                k.wait("scalar", ch_pe, ab_tr_done[b - 2])
            k.wait("vector", ch_xP[sl], xv)
            k.wait("scalar", ch_xP[sl], xv)
            _ln_emit4(k, ch_v, ch_a,
                      [xa[sl][:, g * 1024:(g + 1) * 1024] for g in range(4)],
                      [lnb[sl][:, g * 1024:(g + 1) * 1024] for g in range(4)],
                      smAB[:, :], [sqAB[:, g * 1024:(g + 1) * 1024] for g in range(4)], EPS_LN)
            ab_act_done[b] = ch_a.n

            if ab_stage < 2:
                continue
            k.wait("tensor", ch_a, ab_act_done[b])
            if b >= 2:
                k.wait("vector", ch_pe, ab_z_done[b - 2])
            for dc in range(DCN):
                bank = dc % 2
                k.wait("tensor", ch_v, ptr_last[bank])
                for g in range(4):
                    k.op("tensor",
                         lambda e, dc=dc, g=g, bank=bank, sl=sl:
                         e.transpose(ptr[bank][:, g * 128:(g + 1) * 128],
                                     lnb[sl][:, g * 1024 + dc * 128: g * 1024 + dc * 128 + 128],
                                     ident[:, :]),
                         ch_pe if g == 3 else None)
                trv = ch_pe.n
                k.wait("vector", ch_pe, trv)
                ev = k.op("vector", lambda e, dc=dc, bank=bank, sl=sl:
                          e.tensor_copy(lnf[sl][:, dc * 512:(dc + 1) * 512], ptr[bank][:, :]), ch_v)
                ptr_last[bank] = ev
            ab_tr_done[b] = ch_pe.n

            k.wait("tensor", ch_v, ch_v.n)
            for ec in range(ECN):
                bank = ec % 2
                k.wait("tensor", ch_v, pz_last[bank])
                for dc in range(DCN):
                    last = dc == DCN - 1
                    k.op("tensor",
                         lambda e, dc=dc, ec=ec, bank=bank, sl=sl, last=last:
                         e.matmul(pz[bank][:, :],
                                  win[:, dc * 2048 + ec * 128: dc * 2048 + ec * 128 + 128],
                                  lnf[sl][:, dc * 512:(dc + 1) * 512],
                                  start=(dc == 0), stop=last),
                         ch_pe if last else None)
                k.wait("vector", ch_pe, ch_pe.n)
                if b > 0:
                    k.wait("vector", ch_p, carry.get((b, ec), 0))
                    k.wait("vector", ch_pe, tapmm.get((b - 1, ec), 0))
                ae = k.op("vector",
                          lambda e, ec=ec, bank=bank:
                          e.tensor_copy(zt[:, ec * 516 + 2: ec * 516 + 514], pz[bank][:, :]),
                          ch_v)
                zev[(b, ec)] = ae
                pz_last[bank] = ae
            ab_z_done[b] = ch_pe.n

            if ab_stage < 3:
                continue
            if b >= 2:
                k.wait("vector", ch_s0stP[sl])
                k.wait("vector", ch_v1stP[sl])
            for pr in range(8):
                for side, ec in ((0, pr), (2, pr + 8)):
                    bank = (pr * 2 + (side // 2)) % 2
                    k.wait("tensor", ch_v, zev[(b, ec)])
                    k.wait("tensor", ch_v, pu_last[bank])
                    if b > 0:
                        k.wait("tensor", ch_p, carry.get((b, ec), 0))
                    for kk in range(3):
                        last = kk == 2
                        k.op("tensor",
                             lambda e, ec=ec, kk=kk, bank=bank, last=last:
                             e.matmul(pu[bank][:, :],
                                      cdiag[:, (ec * 3 + kk) * 128:(ec * 3 + kk + 1) * 128],
                                      zt[:, ec * 516 + kk: ec * 516 + kk + 512],
                                      start=(kk == 0), stop=last),
                             ch_pe if last else None)
                    tapmm[(b, ec)] = ch_pe.n
                    k.wait("vector", ch_pe, ch_pe.n)
                    if side == 0:
                        dstu = u0b[sl][:, pr * 512:(pr + 1) * 512]
                    else:
                        dstu = u2t[:, :]
                        if pr > 0:
                            k.wait("vector", ch_v, v1mul[(b, pr - 1)])
                        elif b > 0:
                            k.wait("vector", ch_v, v1mul[(b - 1, 7)])
                    au = k.op("vector",
                              lambda e, ec=ec, bank=bank, dstu=dstu:
                              e.tensor_single_scalar(dstu, pu[bank][:, :], ubias[:, ec:ec + 1], ALU.add),
                              ch_v)
                    uev[(b, ec)] = au
                    pu_last[bank] = au
                    if b == HALO // 512:
                        k.wait("vector", ch_v, au)
                        fxa = k.op("vector", lambda e, dstu=dstu, ec=ec:
                                   e.tensor_tensor(dstu[:, 0:1], dstu[:, 0:1], deln[:, ec:ec + 1], ALU.add), ch_v)
                        k.wait("vector", ch_v, fxa)
                        fxb = k.op("vector", lambda e, dstu=dstu, ec=ec:
                                   e.tensor_tensor(dstu[:, 1:2], dstu[:, 1:2], deln[:, ECN + ec:ECN + ec + 1], ALU.add), ch_v)
                        k.wait("vector", ch_v, fxb)
                if ab_stage < 4:
                    v1mul[(b, pr)] = ch_v.n
                    continue
                k.wait("vector", ch_v, uev[(b, pr + 8)])
                vv = k.op("vector", lambda e, pr=pr, sl=sl:
                          e.tensor_tensor(v1b[sl][:, pr * 512:(pr + 1) * 512],
                                          u0b[sl][:, pr * 512:(pr + 1) * 512], u2t[:, :], ALU.mult), ch_v)
                v1mul[(b, pr)] = vv
            ab_taps_v[b] = ch_v.n

            if b + 1 < nb:
                for ec in range(ECN):
                    k.wait("gpsimd", ch_pe, tapmm[(b, ec)])
                    cc_ = k.op("gpsimd", lambda e, ec=ec:
                               e.tensor_copy(zt[:, ec * 516: ec * 516 + 2],
                                             zt[:, ec * 516 + 512: ec * 516 + 514]), ch_p)
                    carry[(b + 1, ec)] = cc_

            if ab_stage >= 5:
                k.wait("gpsimd", ch_v, ab_taps_v[b])
                k.dma("gpsimd", bass.AP(s_v1, b * 512, [[NPOS, 128], [NPOS * 128, 8], [1, 512]]),
                      v1b[sl][:, :], ch_v1stP[sl])
                k.wait("gpsimd", ch_v, uev[(b, 7)])
                k.dma("gpsimd", bass.AP(s_s0, b * 512, [[NPOS, 128], [NPOS * 128, 8], [1, 512]]),
                      u0b[sl][:, :], ch_s0stP[sl])

        k.wait("sync", ch_v1stP[0])
        k.wait("sync", ch_v1stP[1])
        k.wait("sync", ch_xP[0])
        k.wait("sync", ch_xP[1])
        k.wait("sync", ch_s0stP[0])
        k.wait("sync", ch_s0stP[1])
        k.wait("gpsimd", ch_wu)
        esAB.close()
        k.barrier(chb)

    ch_ringP = [Ch(nc, "ringA"), Ch(nc, "ringB")]
    ch_vfP = [Ch(nc, "vfA"), Ch(nc, "vfB")]
    ch_sfP = [Ch(nc, "sfA"), Ch(nc, "sfB")]
    ch_xrP = [Ch(nc, "xrA"), Ch(nc, "xrB")]
    ch_dbg = Ch(nc, "dbg")
    ch_x1st = Ch(nc, "x1st")
    ch_xn2st = Ch(nc, "xn2st")
    ch_nst = Ch(nc, "nst")
    ch_outP = [Ch(nc, "outA"), Ch(nc, "outB")]

    if "C" in phases:
        esC = ExitStack()
        wout = esC.enter_context(nc.sbuf_tensor("wout", [128, DCN * 1024], BF16))
        ring = esC.enter_context(nc.sbuf_tensor("ring", [128, 16 * 1024], BF16))
        v1f = [esC.enter_context(nc.sbuf_tensor(f"v1f{i}", [128, 8 * 512], BF16)) for i in range(2)]
        s0f = [esC.enter_context(nc.sbuf_tensor(f"s0f{i}", [128, 8 * 512], BF16)) for i in range(2)]
        xr = [esC.enter_context(nc.sbuf_tensor(f"xr{i}", [128, 4096], F32)) for i in range(2)]
        yf = [esC.enter_context(nc.sbuf_tensor(f"yf{i}", [128, 8 * 512], BF16)) for i in range(2)]
        tf = esC.enter_context(nc.sbuf_tensor("tf", [128, 512], F32))
        lnt4 = [esC.enter_context(nc.sbuf_tensor(f"lnt{i}", [128, 1024], F32)) for i in range(4)]
        aot4 = [esC.enter_context(nc.sbuf_tensor(f"aot{i}", [128, 1024], F32)) for i in range(4)]
        x1t4 = [esC.enter_context(nc.sbuf_tensor(f"x1t{i}", [128, 1024], BF16)) for i in range(4)]
        xint4 = [esC.enter_context(nc.sbuf_tensor(f"xint{i}", [128, 1024], BF16)) for i in range(4)]
        nrt4 = [esC.enter_context(nc.sbuf_tensor(f"nrt{i}", [128, 1024], BF16)) for i in range(4)]
        xsc4 = [esC.enter_context(nc.sbuf_tensor(f"xsc{i}", [128, 1024], BF16)) for i in range(4)]
        smC = esC.enter_context(nc.sbuf_tensor("smC", [128, 96], F32))
        pcv = [esC.enter_context(nc.psum_tensor(f"pcv{i}", [128, 512], F32)) for i in range(2)]
        po = [esC.enter_context(nc.psum_tensor(f"po{i}", [128, 512], F32)) for i in range(4)]

        for cc in range(DCN):
            k.dma("gpsimd", wout[:, cc * 1024:(cc + 1) * 1024],
                  w_out_i[cc * 128:(cc + 1) * 128, :], ch_wu)
        wu_wout = ch_wu.n
        k.wait("tensor", ch_wu, wu_wout)

        ring_mask = {}
        tf_last = 0
        conv_done = [0] * NCB
        cv_last = {0: 0, 1: 0}
        po_last = {0: 0, 1: 0, 2: 0, 3: 0}
        epi_v = [0] * NCB
        epi_a = [0] * NCB
        lnA_a = {}
        lnX_a = {}

        def load_ring_group(jcs, par):
            ch = ch_ringP[par]
            for jc in jcs:
                slot = jc % 16
                k.dma("sync", ring[:, slot * 1024:(slot + 1) * 1024],
                      bass.AP(s_v1, jc * 128, [[NPOS, 1024], [1, 128]]),
                      ch, transpose=True)
            k.wait("vector", ch, ch.n)
            for jc in jcs:
                slot = jc % 16
                mv = k.op("vector", lambda e, slot=slot, jc=jc:
                          e.tensor_single_scalar(ring[:, slot * 1024:(slot + 1) * 1024],
                                                 ring[:, slot * 1024:(slot + 1) * 1024],
                                                 cpm[:, jc:jc + 1], ALU.mult), ch_v)
                ring_mask[jc] = mv

        load_ring_group(range(4, 16), 0)
        load_ring_group(range(16, 20), 1)

        for bi in range(NCB):
            sl = bi % 2
            l0 = HALO + 512 * bi
            if bi >= 1 and bi + 1 < NCB:
                k.wait("sync", ch_pe, conv_done[bi - 1])
                load_ring_group(range(16 + 4 * bi, 20 + 4 * bi), (bi + 1) % 2)
            if bi >= 2:
                k.wait("sync", ch_v, epi_v[bi - 2])
            k.dma("sync", v1f[sl][:, :], bass.AP(s_v1, l0, [[NPOS, 128], [NPOS * 128, 8], [1, 512]]), ch_vfP[sl])
            k.dma("sync", s0f[sl][:, :], bass.AP(s_s0, l0, [[NPOS, 128], [NPOS * 128, 8], [1, 512]]), ch_sfP[sl])
            vfv, sfv = ch_vfP[sl].n, ch_sfP[sl].n
            if bi >= 2:
                k.wait("scalar", ch_v, epi_v[bi - 2])
                k.wait("scalar", ch_a, epi_a[bi - 2])
            k.dma("scalar", xr[sl][:, :], bass.AP(xs, l0 * D, [[D, 128], [128 * D, 4], [1, D]]), ch_xrP[sl])
            xrv = ch_xrP[sl].n

            jlist = list(range((l0 - 1023) // 128, (l0 + 511) // 128 + 1))
            for jc in jlist:
                k.wait("tensor", ch_v, ring_mask[jc])
            for cc in range(DCN):
                bank = cc % 2
                k.wait("tensor", ch_v, cv_last[bank])
                for i, jc in enumerate(jlist):
                    off = l0 - 128 * jc + 384
                    ilast = i == len(jlist) - 1
                    k.op("tensor",
                         lambda e, jc=jc, cc=cc, off=off, bank=bank, i=i, ilast=ilast:
                         e.matmul(pcv[bank][:, :],
                                  ring[:, (jc % 16) * 1024 + cc * 128:(jc % 16) * 1024 + cc * 128 + 128],
                                  G[:, off:off + 512],
                                  start=(i == 0), stop=ilast),
                         ch_pe if ilast else None)
                mmv = ch_pe.n
                k.wait("vector", ch_pe, mmv)
                k.wait("vector", ch_vfP[sl], vfv)
                k.wait("vector", ch_sfP[sl], sfv)
                k.wait("vector", ch_v, tf_last)
                e1 = k.op("vector", lambda e, cc=cc, bank=bank, sl=sl:
                          e.scalar_tensor_tensor(tf[:, :], v1f[sl][:, cc * 512:(cc + 1) * 512],
                                                 cfb[:, cc:cc + 1], pcv[bank][:, :],
                                                 ALU.mult, ALU.add), ch_v)
                k.wait("vector", ch_v, e1)
                e2 = k.op("vector", lambda e, cc=cc, sl=sl:
                          e.tensor_tensor(yf[sl][:, cc * 512:(cc + 1) * 512], tf[:, :],
                                          s0f[sl][:, cc * 512:(cc + 1) * 512], ALU.mult), ch_v)
                cv_last[bank] = e2
                tf_last = e2
            conv_done[bi] = ch_pe.n
            yf_done = ch_v.n
            if debug:
                k.wait("sync", ch_v, yf_done)
                k.dma("sync", bass.AP(s_y, bi * 512, [[NOWN, 128], [NOWN * 128, 8], [1, 512]]),
                      yf[sl][:, :], ch_dbg)

            k.wait("tensor", ch_v, yf_done)
            for tg in range(4):
                slg = tg % 2
                for dh in range(2):
                    bank = slg * 2 + dh
                    k.wait("tensor", ch_v, po_last[bank])
                    for cc in range(DCN):
                        k.op("tensor",
                             lambda e, cc=cc, tg=tg, dh=dh, bank=bank, sl=sl:
                             e.matmul(po[bank][:, :],
                                      yf[sl][:, cc * 512 + tg * 128: cc * 512 + tg * 128 + 128],
                                      wout[:, cc * 1024 + dh * 512: cc * 1024 + (dh + 1) * 512],
                                      start=(cc == 0), stop=(cc == DCN - 1)),
                             ch_pe if cc == DCN - 1 else None)
                pov = ch_pe.n
                k.wait("vector", ch_pe, pov)
                if bi >= 1:
                    k.wait("vector", ch_a, lnA_a.get(bi - 1, 0))
                for dh in range(2):
                    bank = slg * 2 + dh
                    bv = k.op("vector", lambda e, dh=dh, bank=bank, tg=tg:
                              e.tensor_tensor(aot4[tg][:, dh * 512:(dh + 1) * 512],
                                              po[bank][:, :], cbout[:, dh * 512:(dh + 1) * 512],
                                              ALU.add), ch_v)
                    po_last[bank] = bv
            bv_last = ch_v.n
            if debug:
                k.wait("sync", ch_v, bv_last)
                for tg in range(4):
                    k.dma("sync", bass.AP(s_attn, (bi * 512 + tg * 128) * D, [[D, 128], [1, D]]),
                          aot4[tg][:, :], ch_dbg)
            k.wait("vector", ch_v, bv_last)
            k.wait("scalar", ch_v, bv_last)
            lnA = _ln_emit4(k, ch_v, ch_a, [t[:, :] for t in aot4], [t[:, :] for t in lnt4],
                            smC[:, 0:32], [t[:, :] for t in xsc4], EPS_LN)
            lnA_a[bi] = lnA
            k.wait("vector", ch_a, lnA)
            k.wait("vector", ch_xrP[sl], xrv)
            k.wait("vector", ch_x1st)
            k.wait("vector", ch_a, lnX_a.get(bi - 1, 0))
            xv1l = 0
            for tg in range(4):
                xv1l = k.op("vector", lambda e, tg=tg, sl=sl:
                            e.tensor_tensor(x1t4[tg][:, :], xr[sl][:, tg * 1024:(tg + 1) * 1024],
                                            lnt4[tg][:, :], ALU.add), ch_v)
            k.wait("scalar", ch_v, xv1l)
            for tg in range(4):
                k.dma("scalar", bass.AP(s_x1, (bi * 512 + tg * 128) * D, [[D, 128], [1, D]]),
                      x1t4[tg][:, :], ch_x1st)
            k.wait("vector", ch_v, xv1l)
            k.wait("scalar", ch_xn2st)
            lnX = _ln_emit4(k, ch_v, ch_a, [t[:, :] for t in x1t4], [t[:, :] for t in xint4],
                            smC[:, 32:64], [t[:, :] for t in xsc4], EPS_LN)
            lnX_a[bi] = lnX
            k.wait("scalar", ch_a, lnX)
            for tg in range(4):
                k.dma("scalar", bass.AP(s_xn2, (bi * 512 + tg * 128) * D, [[D, 128], [1, D]]),
                      xint4[tg][:, :], ch_xn2st)
            smR = smC[:, 64:96]
            aql = 0
            for tg in range(4):
                aql = k.op("scalar", lambda e, tg=tg:
                           e.activation(xsc4[tg][:, :], xint4[tg][:, :], AF.Square,
                                        accum_out=smR[:, tg:tg + 1]), ch_a)
            k.wait("vector", ch_a, aql)
            r1 = k.op("vector", lambda e, smR=smR:
                      e.tensor_scalar(smR[:, 4:8], smR[:, 0:4], 1.0 / D, EPS_RMS,
                                      ALU.mult, ALU.add), ch_v)
            k.wait("scalar", ch_v, r1)
            r2 = k.op("scalar", lambda e, smR=smR: e.activation(smR[:, 8:12], smR[:, 4:8], AF.Ln), ch_a)
            k.wait("scalar", ch_a, r2)
            r3 = k.op("scalar", lambda e, smR=smR:
                      e.activation(smR[:, 8:12], smR[:, 8:12], AF.Exp, scale=-0.5), ch_a)
            k.wait("scalar", ch_a, r3)
            r4l = 0
            for tg in range(4):
                r4l = k.op("scalar", lambda e, tg=tg, smR=smR:
                           e.activation(xsc4[tg][:, :], xint4[tg][:, :], AF.Identity,
                                        scale=smR[:, 8 + tg:9 + tg]), ch_a)
            k.wait("vector", ch_a, r4l)
            k.wait("vector", ch_nst)
            r5l = 0
            for tg in range(4):
                r5l = k.op("vector", lambda e, tg=tg:
                           e.tensor_tensor(nrt4[tg][:, :], xsc4[tg][:, :], cnormw[:, :], ALU.mult), ch_v)
            k.wait("scalar", ch_v, r5l)
            for tg in range(4):
                k.dma("scalar", bass.AP(s_normed, (bi * 512 + tg * 128) * D, [[D, 128], [1, D]]),
                      nrt4[tg][:, :], ch_nst)
            epi_v[bi] = ch_v.n
            epi_a[bi] = ch_a.n

        k.wait("sync", ch_ringP[0])
        k.wait("sync", ch_ringP[1])
        k.wait("sync", ch_vfP[0])
        k.wait("sync", ch_vfP[1])
        k.wait("sync", ch_sfP[0])
        k.wait("sync", ch_sfP[1])
        k.wait("sync", ch_dbg)
        k.wait("scalar", ch_xrP[0])
        k.wait("scalar", ch_xrP[1])
        k.wait("scalar", ch_x1st)
        k.wait("scalar", ch_xn2st)
        k.wait("scalar", ch_nst)
        k.wait("scalar", ch_dbg)
        k.wait("gpsimd", ch_wu)
        esC.close()
        k.barrier(chb)

    ch_ntrP = [Ch(nc, "ntrA"), Ch(nc, "ntrB")]
    ch_gwP = [Ch(nc, "gwA"), Ch(nc, "gwB")]
    ch_dwP = [Ch(nc, "dwA"), Ch(nc, "dwB")]
    ch_x1rP = [Ch(nc, "x1rA"), Ch(nc, "x1rB")]
    ch_xinP = [Ch(nc, "xinA"), Ch(nc, "xinB")]

    if "D" in phases:
        esD = ExitStack()
        nrmB = [[esD.enter_context(nc.sbuf_tensor(f"nrm{p}_{i}", [128, DCN * 512], BF16))
                 for i in range(2)] for p in range(2)]
        guw = [esD.enter_context(nc.sbuf_tensor(f"guw{i}", [128, 2 * DCN * GU_SEC], BF16)) for i in range(2)]
        hid = esD.enter_context(nc.sbuf_tensor("hid", [128, HCN * TB], BF16))
        dnw = [esD.enter_context(nc.sbuf_tensor(f"dnw{i}", [128, HCN * 256], BF16)) for i in range(2)]
        accB = [esD.enter_context(nc.sbuf_tensor(f"acc{p}", [128, (TB // 128) * 1024], BF16))
                for p in range(2)]
        combB = [esD.enter_context(nc.sbuf_tensor(f"comb{p}", [128, (TB // 128) * 8], F32))
                 for p in range(2)]
        cL = esD.enter_context(nc.sbuf_tensor("cL", [128, 8], F32))
        cR = esD.enter_context(nc.sbuf_tensor("cR", [128, 8], F32))
        cE = esD.enter_context(nc.sbuf_tensor("cE", [128, 8], F32))
        smD = esD.enter_context(nc.sbuf_tensor("smD", [128, 16], F32))
        sil = [esD.enter_context(nc.sbuf_tensor(f"sil{i}", [128, 512], F32)) for i in range(3)]
        x1r = [esD.enter_context(nc.sbuf_tensor(f"x1r{i}", [128, 1024], BF16)) for i in range(2)]
        xinE = [esD.enter_context(nc.sbuf_tensor(f"xinE{i}", [128, 1024], BF16)) for i in range(2)]
        ffnE = esD.enter_context(nc.sbuf_tensor("ffnE", [128, 1024], F32))
        lnE = esD.enter_context(nc.sbuf_tensor("lnE", [128, 1024], F32))
        x2E = [esD.enter_context(nc.sbuf_tensor(f"x2E{i}", [128, 1024], F32)) for i in range(2)]
        sqE = esD.enter_context(nc.sbuf_tensor("sqE", [128, 1024], F32))
        smE = esD.enter_context(nc.sbuf_tensor("smE", [128, 16], F32))
        pg = [esD.enter_context(nc.psum_tensor(f"pg{i}", [128, 512], F32)) for i in range(3)]
        pup = [esD.enter_context(nc.psum_tensor(f"pup{i}", [128, 512], F32)) for i in range(2)]
        pd = [esD.enter_context(nc.psum_tensor(f"pd{i}", [128, 256], F32)) for i in range(2)]
        plog = esD.enter_context(nc.psum_tensor("plog", [128, 64], F32))

        NTG = TB // 128
        xv2s = {}
        gu_done_prev = 0
        dn_done_prev = 0
        pg_last = {0: 0, 1: 0, 2: 0}
        pup_last = {0: 0, 1: 0}
        pd_last = {0: 0, 1: 0}
        sil_last = {0: 0, 1: 0, 2: 0}
        plog_read = 0
        dance_a_prev = 0

        ntrv = [0, 0]
        gu_end = {}

        def routing_prep(tb):
            nonlocal plog_read, dance_a_prev
            p = tb % 2
            t0p = tb * TB
            k.wait("sync", ch_pe, gu_end.get(tb - 2, 0))
            for tkb in range(2):
                for dc in range(DCN):
                    k.dma("sync", nrmB[p][tkb][:, dc * 512:(dc + 1) * 512],
                          bass.AP(s_normed, (t0p + tkb * 512) * D + dc * 128, [[D, 512], [1, 128]]),
                          ch_ntrP[p], transpose=True)
            ntrv[p] = ch_ntrP[p].n
            k.wait("tensor", ch_ntrP[p], ntrv[p])
            k.wait("tensor", ch_v, plog_read)
            for tg in range(NTG):
                tkb, tgl = tg // 4, tg % 4
                for dc in range(DCN):
                    k.op("tensor",
                         lambda e, tg=tg, tkb=tkb, tgl=tgl, dc=dc, p=p:
                         e.matmul(plog[:, tg * 8:(tg + 1) * 8],
                                  nrmB[p][tkb][:, dc * 512 + tgl * 128: dc * 512 + tgl * 128 + 128],
                                  crouter[:, dc * 8:(dc + 1) * 8],
                                  start=(dc == 0), stop=(dc == DCN - 1)),
                         ch_pe if dc == DCN - 1 else None)
            lgv = ch_pe.n
            for tg in range(NTG):
                k.wait("vector", ch_pe, lgv)
                k.wait("vector", ch_a, dance_a_prev)
                d1 = k.op("vector", lambda e, tg=tg: e.tensor_copy(cL[:, :], plog[:, tg * 8:(tg + 1) * 8]), ch_v)
                k.wait("vector", ch_v, d1)
                d2 = k.op("vector", lambda e: e.reduce_max(smD[:, 0:1], cL[:, :], axis=AX.X, negate=True), ch_v)
                k.wait("scalar", ch_v, d2)
                d3 = k.op("scalar", lambda e: e.activation(cE[:, :], cL[:, :], AF.Exp, bias=smD[:, 0:1]), ch_a)
                for j in range(8):
                    dj = k.op("vector", lambda e, j=j:
                              e.tensor_single_scalar(cR[:, :], cL[:, :], cL[:, j:j + 1], ALU.is_gt), ch_v)
                    k.wait("vector", ch_v, dj)
                    dj2 = k.op("vector", lambda e, j=j:
                               e.reduce_sum(smD[:, 1 + j:2 + j], cR[:, :], axis=AX.X), ch_v)
                    k.wait("vector", ch_v, dj2)
                d4 = k.op("vector", lambda e:
                          e.tensor_single_scalar(cR[:, :], smD[:, 1:9], 1.5, ALU.is_lt), ch_v)
                k.wait("vector", ch_a, d3)
                k.wait("vector", ch_v, d4)
                d5 = k.op("vector", lambda e: e.tensor_tensor(cE[:, :], cE[:, :], cR[:, :], ALU.mult), ch_v)
                k.wait("vector", ch_v, d5)
                d6 = k.op("vector", lambda e: e.reduce_sum(smD[:, 0:1], cE[:, :], axis=AX.X), ch_v)
                k.wait("vector", ch_v, d6)
                d7 = k.op("vector", lambda e: e.reciprocal(smD[:, 1:2], smD[:, 0:1]), ch_v)
                k.wait("vector", ch_v, d7)
                k.op("vector", lambda e, tg=tg, p=p:
                     e.tensor_single_scalar(combB[p][:, tg * 8:(tg + 1) * 8], cE[:, :],
                                            smD[:, 1:2], ALU.mult), ch_v)
                dance_a_prev = d3
            plog_read = ch_v.n

        def e_chunk(tbe, tg):
            pe_ = tbe % 2
            slg = tg % 2
            gq = tbe * NTG + tg
            row0 = tbe * TB + tg * 128
            if gq >= 2:
                k.wait("sync", ch_v, xv2s[gq - 2])
                k.wait("sync", ch_v, fvs.get(gq - 2, 0))
            k.dma("sync", x1r[slg][:, :], bass.AP(s_x1, row0 * D, [[D, 128], [1, D]]), ch_x1rP[slg])
            k.dma("sync", xinE[slg][:, :], bass.AP(s_xn2, row0 * D, [[D, 128], [1, D]]), ch_xinP[slg])
            x1v = ch_x1rP[slg].n
            k.wait("vector", ch_x1rP[slg], x1v)
            k.wait("vector", ch_xinP[slg], ch_xinP[slg].n)
            fv = k.op("vector", lambda e, tg=tg, pe_=pe_, slg=slg:
                      e.scalar_tensor_tensor(ffnE[:, :], accB[pe_][:, tg * 1024:(tg + 1) * 1024],
                                             calpha[:, 0:1], xinE[slg][:, :], ALU.mult, ALU.add), ch_v)
            fvs[gq] = fv
            k.wait("vector", ch_v, fv)
            k.wait("scalar", ch_v, fv)
            lnret2 = _ln_emit(k, ch_v, ch_a, ffnE[:, :], lnE[:, :], smE[:, 8:16], sqE[:, :], EPS_LN)
            k.wait("vector", ch_a, lnret2[0])
            if gq >= 2:
                k.wait("vector", ch_outP[slg])
            xv2 = k.op("vector", lambda e, slg=slg:
                       e.tensor_tensor(x2E[slg][:, :], x1r[slg][:, :], lnE[:, :], ALU.add), ch_v)
            xv2s[gq] = xv2
            k.wait("scalar", ch_v, xv2)
            k.dma("scalar", bass.AP(y_out, row0 * D, [[D, 128], [1, D]]), x2E[slg][:, :], ch_outP[slg])

        gw_pending = {0: 0, 1: 0}
        fvs = {}

        def issue_gu_sections(en, s, war_val):
            slw = s % 2
            k.wait("gpsimd", ch_pe, war_val)
            for side in range(2):
                hoff = side * HID + s * GU_SEC
                k.dma("gpsimd", guw[slw][:, side * DCN * GU_SEC:(side + 1) * DCN * GU_SEC],
                      bass.AP(gu_T, en * D * 2 * HID + hoff,
                              [[2 * HID, 128], [128 * 2 * HID, DCN], [1, GU_SEC]]),
                      ch_gwP[slw])
            return ch_gwP[slw].n

        gw_pending[0] = issue_gu_sections(0, 0, 0)
        gw_pending[1] = issue_gu_sections(0, 1, 0)
        routing_prep(0)
        for tb in range(ntb):
            p = tb % 2
            nrm = nrmB[p]
            comb = combB[p]
            acc = accB[p]
            for e_ in range(nexp):
                if d_stage < 2:
                    continue
                hse = 0
                sec_end = {}
                for s in range(NSEC):
                    slw = s % 2
                    if s >= 2:
                        gw_pending[slw] = issue_gu_sections(e_, s, sec_end[s - 2])
                    gwv = gw_pending[slw]
                    k.wait("tensor", ch_gwP[slw], gwv)
                    for tkb in range(2):
                        for hcl in range(2):
                            hc = s * 2 + hcl
                            par = hse % 2
                            pr3 = hse % 3
                            hse += 1
                            k.wait("tensor", ch_a, pg_last[pr3])
                            for dc in range(DCN):
                                k.op("tensor",
                                     lambda e, dc=dc, hcl=hcl, tkb=tkb, par=pr3, slw=slw, p=p:
                                     e.matmul(pg[par][:, :],
                                              guw[slw][:, dc * GU_SEC + hcl * 128: dc * GU_SEC + hcl * 128 + 128],
                                              nrmB[p][tkb][:, dc * 512:(dc + 1) * 512],
                                              start=(dc == 0), stop=(dc == DCN - 1)),
                                     ch_pe if dc == DCN - 1 else None)
                            k.wait("tensor", ch_v, pup_last[par])
                            for dc in range(DCN):
                                k.op("tensor",
                                     lambda e, dc=dc, hcl=hcl, tkb=tkb, par=par, slw=slw, p=p:
                                     e.matmul(pup[par][:, :],
                                              guw[slw][:, DCN * GU_SEC + dc * GU_SEC + hcl * 128:
                                                       DCN * GU_SEC + dc * GU_SEC + hcl * 128 + 128],
                                              nrmB[p][tkb][:, dc * 512:(dc + 1) * 512],
                                              start=(dc == 0), stop=(dc == DCN - 1)),
                                     ch_pe if dc == DCN - 1 else None)
                            guv = ch_pe.n
                            gu_sec_done = guv
                            k.wait("scalar", ch_pe, guv)
                            k.wait("scalar", ch_v, sil_last[pr3])
                            av = k.op("scalar", lambda e, pr3=pr3:
                                      e.activation(sil[pr3][:, :], pg[pr3][:, :], AF.Silu), ch_a)
                            pg_last[pr3] = av
                            k.wait("vector", ch_a, av)
                            k.wait("vector", ch_pe, dn_done_prev)
                            hv = k.op("vector", lambda e, par=par, pr3=pr3, hc=hc, tkb=tkb:
                                      e.tensor_tensor(hid[:, hc * TB + tkb * 512: hc * TB + tkb * 512 + 512],
                                                      sil[pr3][:, :], pup[par][:, :], ALU.mult), ch_v)
                            pup_last[par] = hv
                            sil_last[pr3] = hv
                    sec_end[s] = ch_pe.n
                gu_done_prev = ch_pe.n
                hid_done = ch_v.n
                if e_ == nexp - 1:
                    gu_end[tb] = gu_done_prev

                if d_stage < 3:
                    continue
                dn_q_end = {}
                for q in range(4):
                    slq = q % 2
                    if q == 2 and (e_ + 1 < nexp or tb + 1 < ntb):
                        en = (e_ + 1) % nexp
                        gw_pending[0] = issue_gu_sections(en, 0, gu_done_prev)
                        gw_pending[1] = issue_gu_sections(en, 1, gu_done_prev)
                    k.wait("gpsimd", ch_pe, dn_q_end[q - 2] if q >= 2 else dn_done_prev)
                    k.dma("gpsimd", dnw[slq][:, :],
                          bass.AP(dn_T, e_ * HID * D + q * 256,
                                  [[D, 128], [128 * D, HCN], [1, 256]]),
                          ch_dwP[slq])
                    dwv = ch_dwP[slq].n
                    k.wait("tensor", ch_dwP[slq], dwv)
                    k.wait("tensor", ch_v, hid_done)
                    for tg in range(NTG):
                        par = (q * NTG + tg) % 2
                        k.wait("tensor", ch_v, pd_last[par])
                        for hc in range(HCN):
                            k.op("tensor",
                                 lambda e, hc=hc, tg=tg, par=par, slq=slq:
                                 e.matmul(pd[par][:, :],
                                          hid[:, hc * TB + tg * 128: hc * TB + tg * 128 + 128],
                                          dnw[slq][:, hc * 256:(hc + 1) * 256],
                                          start=(hc == 0), stop=(hc == HCN - 1)),
                                 ch_pe if hc == HCN - 1 else None)
                        dn_q_done = ch_pe.n
                        k.wait("vector", ch_pe, dn_q_done)
                        accv = acc[:, tg * 1024 + q * 256: tg * 1024 + (q + 1) * 256]
                        cc_ = comb[:, tg * 8 + e_: tg * 8 + e_ + 1]
                        if e_ == 0:
                            uv = k.op("vector", lambda e, accv=accv, cc_=cc_, par=par:
                                      e.tensor_single_scalar(accv, pd[par][:, :], cc_, ALU.mult), ch_v)
                        else:
                            uv = k.op("vector", lambda e, accv=accv, cc_=cc_, par=par:
                                      e.scalar_tensor_tensor(accv, pd[par][:, :], cc_, accv,
                                                             ALU.mult, ALU.add), ch_v)
                        pd_last[par] = uv
                    dn_q_end[q] = ch_pe.n
                dn_done_prev = ch_pe.n

                if d_stage >= 4:
                    if e_ == 0 and tb + 1 < ntb:
                        routing_prep(tb + 1)
                    if tb > 0:
                        e_chunk(tb - 1, e_)

            if d_stage >= 4 and tb == ntb - 1:
                for tg in range(NTG):
                    e_chunk(tb, tg)

        k.wait("scalar", ch_outP[0])
        k.wait("scalar", ch_outP[1])
        k.wait("sync", ch_ntrP[0])
        k.wait("sync", ch_ntrP[1])
        k.wait("sync", ch_x1rP[0])
        k.wait("sync", ch_x1rP[1])
        k.wait("sync", ch_xinP[0])
        k.wait("sync", ch_xinP[1])
        k.wait("gpsimd", ch_gwP[0])
        k.wait("gpsimd", ch_gwP[1])
        k.wait("gpsimd", ch_dwP[0])
        k.wait("gpsimd", ch_dwP[1])
        esD.close()
    else:
        esY = ExitStack()
        yz = esY.enter_context(nc.sbuf_tensor("yz", [128, 1024], F32))
        vz2 = k.op("vector", lambda e: e.memset(yz[:, :], 0.0), ch_v)
        k.wait("scalar", ch_v, vz2)
        for r in range(NOWN // 128):
            k.dma("scalar", bass.AP(y_out, r * 128 * D, [[D, 128], [1, D]]), yz[:, :], ch_outP[0])
        k.wait("scalar", ch_outP[0])
        esY.close()

    k.replay()
    es_res.close()
    return nc


def _host_prep(inputs):
    x = np.asarray(inputs["L0_attn_x"], dtype=np.float32)
    w_in = np.asarray(inputs["L0_attn_w_in"], np.float32)
    b_in = np.asarray(inputs["L0_attn_b_in"], np.float32)
    w_conv = np.asarray(inputs["L0_attn_w_conv"], np.float32)
    b_conv = np.asarray(inputs["L0_attn_b_conv"], np.float32)
    b1 = np.asarray(inputs["L0_attn_b1"], np.float32)
    w2 = np.asarray(inputs["L0_attn_w2"], np.float32)
    b2 = np.asarray(inputs["L0_attn_b2"], np.float32)
    w3 = np.asarray(inputs["L0_attn_w3"], np.float32)
    fb = np.asarray(inputs["L0_attn_filt_bias"], np.float32)
    w_out = np.asarray(inputs["L0_attn_w_out"], np.float32)
    b_out = np.asarray(inputs["L0_attn_b_out"], np.float32)
    norm_w = np.asarray(inputs["L0_ffn_norm_weight"], np.float32)
    router = np.asarray(inputs["L0_ffn_router_weight"], np.float32)
    gu = np.asarray(inputs["L0_ffn_gate_up_weight"], np.float32)
    dn = np.asarray(inputs["L0_ffn_down_weight"], np.float32)
    alpha = np.asarray(inputs["L0_ffn_residual_alpha"], np.float32).reshape(1, 1)

    sel = np.concatenate([np.arange(0, 1024), np.arange(2048, 3072)])
    w_in_sel = np.ascontiguousarray(w_in[:, sel])
    b_in_sel = np.ascontiguousarray(b_in[sel].reshape(ECN, 128).T)
    wc = w_conv[:, 0, :][sel]
    wconv_sel = np.ascontiguousarray(wc.reshape(ECN, 128, 3).transpose(1, 0, 2).reshape(128, ECN * 3))
    b_conv_sel = np.ascontiguousarray(b_conv[sel].reshape(ECN, 128).T)
    fb_pc = np.ascontiguousarray(fb.reshape(8, 128).T)
    router_Tn = np.ascontiguousarray(router.T)
    gu_Tn = np.ascontiguousarray(gu.transpose(0, 2, 1))
    dn_Tn = np.ascontiguousarray(dn.transpose(0, 2, 1))
    rev = np.zeros((128, 128), np.float32)
    rev[np.arange(128), 127 - np.arange(128)] = 1.0
    diag = np.zeros((ECN * 3, 128, 128), np.float32)
    ii = np.arange(128)
    for ec in range(ECN):
        for kk in range(3):
            diag[ec * 3 + kk, ii, ii] = wc[ec * 128 + ii, kk]

    shared = {
        "w_in_sel": w_in_sel, "b_in_sel": b_in_sel, "wconv_sel": wconv_sel,
        "b_conv_sel": b_conv_sel, "b1c": np.ascontiguousarray(b1.reshape(64, 1)),
        "b2c": np.ascontiguousarray(b2.reshape(64, 1)), "w2i": w2, "w3i": w3,
        "fb_pc": fb_pc, "w_out_i": w_out, "b_out_i": b_out,
        "norm_w_i": norm_w, "router_T": router_Tn, "gu_T": gu_Tn, "dn_T": dn_Tn,
        "alpha_i": alpha, "rev_i": rev, "diagw": diag,
    }
    in_maps = []
    for c in range(8):
        bb, half = c // 2, c % 2
        xsn = np.zeros((NPOS, D), np.float32)
        if half == 0:
            xsn[HALO:] = x[bb, 0:NOWN]
            m = np.zeros(NPOS, np.float32)
            m[HALO:] = 1.0
        else:
            xsn[:, :] = x[bb, NOWN - HALO: 2 * NOWN]
            m = np.ones(NPOS, np.float32)
        pm = np.ascontiguousarray(m.reshape(NPOS // 128, 128).T)
        d = {"xs": xsn, "pmask": pm,
             "flag_i": np.full((1, 1), 1.0 if half == 0 else 0.0, np.float32)}
        d.update(shared)
        in_maps.append(d)
    return in_maps


def kernel(**inputs):
    nc = build_nc()
    in_maps = _host_prep(inputs)
    res = run_bass_kernel_spmd(nc, in_maps, core_ids=list(range(8)))
    out = np.empty((4, 8192, 1024), np.float32)
    for c in range(8):
        bb, half = c // 2, c % 2
        out[bb, half * NOWN:(half + 1) * NOWN] = np.asarray(res.results[c]["y_out"])
    return out
